# version 08
# baseline (speedup 1.0000x reference)
import contextlib
import numpy as np

import concourse.bass as bass
import concourse.mybir as mybir
from concourse.bass_utils import run_bass_kernel_spmd

F32 = mybir.dt.float32
BF16 = mybir.dt.bfloat16
AF = mybir.ActivationFunctionType
ALU = mybir.AluOpType
AX = mybir.AxisListType.X

D = 2048
NH = 16
KT = 16
TC = 1024
CH = 256
NCH = 4
XC = 1027
NE = 8
SHB = 4
FT = 11
BIG = 1.0e9

NLANES = 6
CH_ENGINE = {
    "sync": "sync", "tensor": "tensor", "vector": "vector",
    "scalar": "scalar", "gpsimd": "gpsimd", "gdma": "gpsimd", "sdma": "scalar",
}
DMA_CH = ("sync", "gdma", "sdma")
for _c in DMA_CH:
    for _l in range(NLANES):
        CH_ENGINE[f"{_c}{_l}"] = CH_ENGINE[_c]
LANE_CH = tuple(f"{c}{l}" for c in DMA_CH for l in range(NLANES))


class Prog:

    SELF_WAIT = True

    def __init__(self):
        self.eng_tasks = {e: [] for e in ("sync", "tensor", "vector", "scalar", "gpsimd")}
        self.cnt = {c: 0 for c in CH_ENGINE}

    guard = True

    def task(self, ch, fn, after=(), sw=None):
        if sw is None:
            sw = self.guard
        deps = []
        for t in after:
            if t is None:
                continue
            if isinstance(t, list):
                deps.extend(x for x in t if x is not None)
            else:
                deps.append(t)
        if ch in DMA_CH:
            n = self.cnt[ch]
            lane = n % NLANES
            lch = f"{ch}{lane}"
            gen = self.cnt[lch] // 16
            if gen > 0:
                deps.append((lch, 16 * gen))
            self.cnt[ch] = n + 1
            self.cnt[lch] += 16
            self.eng_tasks[CH_ENGINE[ch]].append((lch, fn, deps, 16, sw))
            return (lch, self.cnt[lch])
        self.cnt[ch] += 1
        self.eng_tasks[CH_ENGINE[ch]].append((ch, fn, deps, 1, sw))
        return (ch, self.cnt[ch])

    def now(self, ch):
        if ch in DMA_CH:
            return [(f"{ch}{l}", self.cnt[f"{ch}{l}"])
                    for l in range(NLANES) if self.cnt[f"{ch}{l}"] > 0]
        return (ch, self.cnt[ch])

    def frontier(self):
        toks = []
        for c in list(DMA_CH) + ["tensor", "vector", "scalar", "gpsimd"]:
            t = self.now(c)
            toks.extend(t if isinstance(t, list) else ([t] if t[1] > 0 else []))
        return toks

    def barrier(self, exempt=("gdma",)):
        toks = []
        for c in list(DMA_CH) + ["tensor", "vector", "scalar", "gpsimd"]:
            if c in exempt:
                continue
            t = self.now(c)
            toks.extend(t if isinstance(t, list) else ([t] if t[1] > 0 else []))
        for eng in self.eng_tasks:
            self.eng_tasks[eng].append((None, None, list(toks), 0, True))

    def build(self, block, sems):
        for eng_name, tasks in self.eng_tasks.items():
            if not tasks:
                continue

            def mk(eng_name, tasks):
                def emit(eng):
                    seen = {}
                    cnt = {}
                    for ch, fn, deps, inc, sw in tasks:
                        if Prog.SELF_WAIT and ch in ("vector", "scalar", "gpsimd") \
                                and cnt.get(ch, 0) > 0 and seen.get(ch, 0) < cnt[ch]:
                            eng.wait_ge(sems[ch], cnt[ch])
                            seen[ch] = cnt[ch]
                        for (dch, dcnt) in deps:
                            if CH_ENGINE[dch] == eng_name and dch not in DMA_CH \
                                    and dch not in LANE_CH:
                                continue
                            if seen.get(dch, 0) >= dcnt:
                                continue
                            eng.wait_ge(sems[dch], dcnt)
                            seen[dch] = dcnt
                        if fn is not None:
                            fn(eng).then_inc(sems[ch], inc)
                            cnt[ch] = cnt.get(ch, 0) + inc
                return emit

            getattr(block, eng_name)(mk(eng_name, tasks))

    def finish(self):
        self.eng_tasks["sync"].append((None, None, self.frontier(), 0, True))


class Arena:
    def __init__(self, t, total):
        self.t, self.total, self.off, self.marks = t, total, 0, []

    def alloc(self, cols, rows=128):
        a = self.off
        self.off += cols
        assert self.off <= self.total, f"arena overflow {self.off}>{self.total}"
        return self.t[0:rows, a:a + cols]

    def push(self):
        self.marks.append(self.off)

    def pop(self):
        self.off = self.marks.pop()


def bcast(col, n):
    return col.to_broadcast([col.shape[0], n])


def build_nc(tap=False):
    nc = bass.Bass("TRN2", target_bir_lowering=False)
    di = {}

    def dinp(name, shape):
        di[name] = nc.dram_tensor(name, shape, F32, kind="ExternalInput").ap()

    dinp("xt", [128, KT * XC])
    dinp("xp", [128, KT * XC])
    dinp("w_in", [128, NH * KT * 3 * 128])
    dinp("w_rt", [128, KT * 8])
    dinp("w_sg", [128, KT * 1])
    dinp("w_shgu", [128, SHB * FT * 2 * KT * 128])
    dinp("w_shdn", [128, SHB * 16 * FT * 128])
    dinp("w_egu", [128, NE * FT * 2 * KT * 128])
    dinp("w_edn", [128, NE * 16 * FT * 128])
    dinp("w_mb", [128, 16 * KT * 128])
    dinp("w_out", [128, 16 * KT * 128])
    dinp("cw", [128, 16 * 4])
    dinp("cb", [128, 16])
    dinp("nsc", [128, 16])
    dinp("dtb", [128, 16])
    dinp("mflag", [128, 1])
    dinp("iota8", [128, 8])
    dinp("ident", [128, 128])
    dinp("esel", [8, 1024])
    dinp("rope", [128, 8 * 1024])

    y_d = nc.dram_tensor("y", [128, KT * TC], F32, kind="ExternalOutput").ap()
    skind = dict(kind="ExternalOutput") if tap else {}
    scr_co = nc.dram_tensor("scr_co", [128, NH * TC], F32, **skind).ap()
    scr_el = nc.dram_tensor("scr_el", [128, NH * TC], F32, **skind).ap()
    scr_gz = nc.dram_tensor("scr_gz", [128, NH * TC], F32, **skind).ap()
    tp = {}
    if tap:
        for name, shape, dt in [
            ("t_xnbf", [128, KT * TC], BF16), ("t_xc", [128, KT * TC], BF16),
            ("t_rows", [1, 2 * XC], F32), ("t_bdbo", [128, 176], F32),
            ("t_cmb", [128, 64], F32), ("t_acc", [128, KT * TC], F32),
            ("t_fin", [128, KT * TC], BF16), ("t_vmg", [128, KT * TC], BF16),
            ("t_shonly", [128, KT * TC], F32),
        ]:
            tp[name] = nc.dram_tensor(name, shape, dt, kind="ExternalOutput").ap()

    stack = contextlib.ExitStack()
    sb = lambda name, shp, dt=F32: stack.enter_context(nc.sbuf_tensor(name, shp, dt))
    psb = lambda name, shp: stack.enter_context(nc.psum_tensor(name, shp, F32))

    with stack:
        arf = Arena(sb("arf", [128, 23800], F32), 23800)
        arb = Arena(sb("arb", [128, 34816], BF16), 34816)
        wg = [sb("wg0", [128, 6144], BF16), sb("wg1", [128, 6144], BF16)]
        wd = [sb("wd0", [128, 1408], BF16), sb("wd1", [128, 1408], BF16)]
        cw = sb("cw_t", [128, 64]); cb = sb("cb_t", [128, 16])
        nsc = sb("nsc_t", [128, 16]); dtb = sb("dtb_t", [128, 16])
        mflag = sb("mflag_t", [128, 1]); m1f = sb("m1f_t", [128, 1])
        iota = sb("iota_t", [128, 8]); ident = sb("ident_t", [128, 128])
        ones_f = sb("ones_f", [1, 128], F32)
        ones_bf = sb("ones_bf", [128, 1], BF16)
        ones_f1 = sb("ones_f1", [128, 1], F32)
        bd_own = sb("bd_own", [128, 64]); bo_own = sb("bo_own", [128, 64])
        pstat = sb("pstat", [128, 48])
        m_row = sb("m_row", [1, XC]); r_row = sb("r_row", [1, XC])
        kc = sb("kconst", [128, 6], F32)
        wrt_f = sb("wrt_f", [128, 128], F32)
        lg_s = sb("lg_s", [8, 1024], F32)
        xnbf = arb.alloc(KT * TC)
        xcbf = arb.alloc(KT * TC)
        spare_bf = arb.alloc(2048)

        PS = [psb("ps%d" % i, [128, 512]) for i in range(6)]
        PR0 = psb("pr0", [1, 512])
        PR1 = psb("pr1", [1, 512])

        sems = {}
        with contextlib.ExitStack() as semstack:
            for c in CH_ENGINE:
                sems[c] = semstack.enter_context(nc.semaphore("s_" + c))
            with nc.Block() as block:
                p = Prog()
                _emit(p, tap, locals())
                p.finish()
                p.build(block, sems)
    return nc


def _emit(p, tap, g):
    di = g["di"]
    arf = g["arf"]
    PS, PR0, PR1 = g["PS"], g["PR0"], g["PR1"]
    wg, wd = g["wg"], g["wd"]
    xnbf, xcbf, spare_bf = g["xnbf"], g["xcbf"], g["spare_bf"]
    cw, cb, nsc, dtb = g["cw"], g["cb"], g["nsc"], g["dtb"]
    mflag, m1f, iota, ident = g["mflag"], g["m1f"], g["iota"], g["ident"]
    ones_f, ones_bf, ones_f1 = g["ones_f"], g["ones_bf"], g["ones_f1"]
    bd_own, bo_own, pstat = g["bd_own"], g["bo_own"], g["pstat"]
    m_row, r_row = g["m_row"], g["r_row"]
    y_d, scr_co, scr_el, scr_gz, tp = g["y_d"], g["scr_co"], g["scr_el"], g["scr_gz"], g["tp"]

    wrt_f, lg_s = g["wrt_f"], g["lg_s"]
    for t, name in [(cw, "cw"), (cb, "cb"), (nsc, "nsc"), (dtb, "dtb"),
                    (mflag, "mflag"), (iota, "iota8"), (ident, "ident"),
                    (wrt_f, "w_rt")]:
        p.task("sync", lambda e, t=t, n=name: e.dma_start(out=t[:, :], in_=di[n][:, :]))
    consts = p.now("sync")
    kc = g["kc"]
    for i, v in [(0, 1e-5), (1, 1e-6), (2, 1.0), (3, -0.5), (4, 0.0)]:
        p.task("vector", lambda e, i=i, v=v: e.memset(kc[:, i:i + 1], v))
    p.task("vector", lambda e: e.memset(ones_f[:, :], 1.0))
    p.task("vector", lambda e: e.memset(ones_bf[:, :], 1.0))
    p.task("vector", lambda e: e.memset(ones_f1[:, :], 1.0))
    p.task("vector", lambda e: e.tensor_scalar(m1f[:, :], mflag[:, :], -1.0, 1.0,
                                               ALU.mult, ALU.add), after=[consts])

    arf.push()
    rope = {}
    rtmp = arf.alloc(1024)
    rfold = None
    for i, nm in enumerate(["ct", "st", "ctp", "stp"]):
        rope[nm] = arf.alloc(1024)
        t1 = p.task("sync", lambda e, i=i, nm=nm: e.dma_start(
            out=rope[nm][:, :], in_=di["rope"][:, i * 1024:(i + 1) * 1024]))
        t2 = p.task("sync", lambda e, i=i: e.dma_start(
            out=rtmp[:, :], in_=di["rope"][:, (4 + i) * 1024:(5 + i) * 1024]),
            after=[rfold])
        rfold = p.task("vector", lambda e, nm=nm: e.tensor_tensor(
            rope[nm][:, :], rope[nm][:, :], rtmp[:, :], ALU.add), after=[t1, t2])

    def phase_p0(xname, own):
        p.guard = False
        arf.push()
        xchunk = [arf.alloc(KT * 259), arf.alloc(KT * 259)]
        bcm = [arf.alloc(259), arf.alloc(259)]
        bcr = [arf.alloc(259), arf.alloc(259)]
        bcrr = [arf.alloc(259), arf.alloc(259)]
        t_sq = [arf.alloc(259), arf.alloc(259)]
        t_px = [arf.alloc(259), arf.alloc(259)]
        t_a = [arf.alloc(259), arf.alloc(259)]
        t_xn2 = [arf.alloc(259), arf.alloc(259)]
        t_cv = [arf.alloc(256), arf.alloc(256)]
        rows_m = arf.alloc(259, rows=1); rows_v = arf.alloc(259, rows=1)
        rows_r = arf.alloc(259, rows=1); rows_u = arf.alloc(259, rows=1)
        rows_t = arf.alloc(259, rows=1)
        x3 = di[xname].rearrange("p (kt t) -> p kt t", kt=KT)

        prev_apply = [None, None]
        sq_use = [None, None]
        row_evac = None
        bc_use = [None, None]
        rt_guard = [None]
        prev_g1 = [(), ()]
        for c in range(NCH):
            par = c % 2
            c0 = c * 256
            loads = []
            last_mm = None
            for dt in range(KT):
                bi = dt % 2
                lx = p.task("sync" if dt % 2 == 0 else "sdma",
                            lambda e, par=par, dt=dt, c0=c0: e.dma_start(
                    out=xchunk[par][:, dt * 259:(dt + 1) * 259],
                    in_=x3[:, dt, c0:c0 + 259]), after=[prev_apply[par]])
                loads.append(lx)
                sq = p.task("scalar", lambda e, par=par, dt=dt, bi=bi: e.activation(
                    t_sq[bi][:, :], xchunk[par][:, dt * 259:(dt + 1) * 259], AF.Square),
                    after=[lx, sq_use[bi]])
                p.task("tensor", lambda e, par=par, dt=dt: e.matmul(
                    PR0[0:1, 0:259], ones_f1[:, :], xchunk[par][:, dt * 259:(dt + 1) * 259],
                    start=(dt == 0), stop=(dt == KT - 1)),
                    after=[lx, row_evac, consts])
                last_mm = p.task("tensor", lambda e, bi=bi, dt=dt: e.matmul(
                    PR1[0:1, 0:259], ones_f1[:, :], t_sq[bi][:, :],
                    start=(dt == 0), stop=(dt == KT - 1)), after=[sq])
                sq_use[bi] = last_mm
            p.task("vector", lambda e: e.tensor_scalar(
                rows_m[:, :], PR0[0:1, 0:259], 1.0 / 2048, None, ALU.mult),
                after=[last_mm])
            row_evac = p.task("vector", lambda e: e.tensor_scalar(
                rows_v[:, :], PR1[0:1, 0:259], 1.0 / 2048, None, ALU.mult))
            p.task("vector", lambda e: e.tensor_tensor(
                rows_t[:, :], rows_m[:, :], rows_m[:, :], ALU.mult))
            p.task("vector", lambda e: e.tensor_tensor(
                rows_v[:, :], rows_v[:, :], rows_t[:, :], ALU.subtract))
            vtok = p.now("vector")
            p.task("scalar", lambda e: e.activation(
                rows_r[:, :], rows_v[:, :], AF.Ln, bias=kc[0:1, 0:1]), after=[vtok])
            s2 = p.task("scalar", lambda e: e.activation(
                rows_r[:, :], rows_r[:, :], AF.Exp, scale=kc[0:1, 3:4]))
            p.task("vector", lambda e: e.tensor_tensor(
                rows_u[:, :], rows_r[:, :], rows_r[:, :], ALU.mult), after=[s2])
            u2 = p.task("vector", lambda e: e.tensor_tensor(
                rows_u[:, :], rows_u[:, :], rows_v[:, :], ALU.mult))
            p.task("scalar", lambda e: e.activation(
                rows_u[:, :], rows_u[:, :], AF.Ln, bias=kc[0:1, 1:2]), after=[u2])
            s4 = p.task("scalar", lambda e: e.activation(
                rows_u[:, :], rows_u[:, :], AF.Exp, scale=kc[0:1, 3:4]))
            p.task("vector", lambda e: e.tensor_tensor(
                rows_u[:, :], rows_u[:, :], rows_r[:, :], ALU.mult), after=[s4])
            if own:
                p.task("vector", lambda e, c0=c0: e.tensor_copy(
                    m_row[:, c0:c0 + 259], rows_m[:, :]))
                p.task("vector", lambda e, c0=c0: e.tensor_copy(
                    r_row[:, c0:c0 + 259], rows_r[:, :]))
            rows_done = p.now("vector")
            bt = []
            for psi, row, dst in [(3, rows_m, bcm[par]), (4, rows_r, bcr[par]),
                                  (5, rows_u, bcrr[par])]:
                mm = p.task("tensor", lambda e, psi=psi, row=row: e.matmul(
                    PS[psi][:, 0:259], ones_f[:, :], row[:, :], start=True, stop=True),
                    after=[rows_done, bc_use[par]])
                bt.append(p.task("vector", lambda e, psi=psi, dst=dst: e.tensor_copy(
                    dst[:, :], PS[psi][:, 0:259]), after=[mm]))
            bc_done = bt[-1]
            bc_use[par] = bc_done
            prev_sil = [None, None]
            bcm_c, bcr_c, bcrr_c = bcm[par], bcr[par], bcrr[par]
            for dt in range(KT):
                bi = dt % 2
                xsl = xchunk[par][:, dt * 259:(dt + 1) * 259]
                g1 = p.task("gpsimd", lambda e, xsl=xsl, bi=bi, bcm_c=bcm_c: e.tensor_tensor(
                    t_px[bi][:, :], xsl, bcm_c[:, :], ALU.subtract),
                    after=[bc_done, loads[dt]] + list(prev_g1[bi]))
                p.task("gpsimd", lambda e, bi=bi, bcrr_c=bcrr_c: e.tensor_tensor(
                    t_a[bi][:, :], t_px[bi][:, :], bcrr_c[:, :], ALU.mult))
                p.task("gpsimd", lambda e, dt=dt, bi=bi: e.tensor_tensor(
                    t_a[bi][:, :], t_a[bi][:, :], bcast(nsc[:, dt:dt + 1], 259), ALU.mult))
                gt = p.now("gpsimd")
                cl = p.task("vector", lambda e, bi=bi: e.tensor_scalar(
                    t_xn2[bi][:, :], t_a[bi][:, :], 60000.0, -60000.0, ALU.min, ALU.max),
                    after=[gt])
                cv0 = p.task("gpsimd", lambda e, dt=dt, bi=bi: e.tensor_tensor(
                    t_cv[bi][:, :], t_xn2[bi][:, 0:256], bcast(cw[:, dt * 4:dt * 4 + 1], 256),
                    ALU.mult), after=[cl, prev_sil[bi]])
                cvb = None
                for k in (1, 2, 3):
                    cvb = p.task("vector", lambda e, dt=dt, k=k, bi=bi: e.scalar_tensor_tensor(
                        t_cv[bi][:, :], t_xn2[bi][:, k:k + 256], cw[:, dt * 4 + k:dt * 4 + k + 1],
                        t_cv[bi][:, :], ALU.mult, ALU.add), after=[cv0])
                prev_sil[bi] = p.task("scalar", lambda e, dt=dt, c0=c0, bi=bi: e.activation(
                    xcbf[:, dt * TC + c0:dt * TC + c0 + 256], t_cv[bi][:, :], AF.Silu,
                    bias=cb[:, dt:dt + 1]), after=[cvb])
                if own:
                    xnc = p.task("vector", lambda e, dt=dt, c0=c0, bi=bi, bcr_c=bcr_c: e.tensor_tensor(
                        xnbf[:, dt * TC + c0:dt * TC + c0 + 256], t_px[bi][:, 3:259],
                        bcr_c[:, 3:259], ALU.mult))
                    xf32 = p.task("vector", lambda e, bi=bi, bcr_c=bcr_c: e.tensor_tensor(
                        t_a[bi][:, 0:256], t_px[bi][:, 3:259], bcr_c[:, 3:259], ALU.mult),
                        after=[cl])
                    rt_mm = p.task("tensor", lambda e, dt=dt, bi=bi: e.matmul(
                        PS[0][0:8, 0:256], wrt_f[:, dt * 8:(dt + 1) * 8],
                        t_a[bi][:, 0:256], start=(dt == 0), stop=(dt == KT - 1)),
                        after=[xf32, rt_guard[0]])
                    prev_g1[bi] = (xnc, rt_mm)
                else:
                    prev_g1[bi] = (cl,)
            if own:
                rt_guard[0] = p.task("vector", lambda e, c0=c0: e.tensor_copy(
                    lg_s[0:8, c0:c0 + 256], PS[0][0:8, 0:256]), after=[prev_g1[1][1]])
            prev_apply[par] = p.now("scalar")
        arf.pop()
        p.guard = True

    def phase_front(own):
        p.guard = False
        arf.push()
        TL = []
        for _ in range(2):
            t = dict(dtt=arf.alloc(TC), tmpa=arf.alloc(TC), tmpb=arf.alloc(TC),
                     vv=arf.alloc(TC), vsw=arf.alloc(TC), vr=arf.alloc(TC),
                     Lt=arf.alloc(TC), eLs=arf.alloc(TC), co=arf.alloc(TC),
                     sm=arf.alloc(40))
            TL.append(t)
        ct = rope["ct" if own else "ctp"]
        stt = rope["st" if own else "stp"]
        nblk = 3 if own else 2
        wtok = [None, None]
        use_tok = [None, None]
        ps_free = [None] * 6
        pv = [dict(vin=None, el=None, el_dma=None, co_dma=None, gz_dma=None,
                   swp=None, swp2=None, r1=None, pr=None, cot=None, gz1=None) for _ in range(2)]
        for h in range(NH):
            par = h % 2
            T = TL[par]
            dtt, tmpa, tmpb = T["dtt"], T["tmpa"], T["tmpb"]
            vv, vsw, vr = T["vv"], T["vsw"], T["vr"]
            Lt, eLs, co, sm = T["Lt"], T["eLs"], T["co"], T["sm"]
            l0 = sm[:, 0:4]; nl0 = sm[:, 4:8]; pbd = sm[:, 8:12]; pbo = sm[:, 12:16]
            pc4 = sm[:, 16:20]; e4 = sm[:, 20:24]; en4 = sm[:, 24:28]
            sd4 = sm[:, 28:32]
            V = pv[par]
            wtok[par] = p.task("gdma", lambda e, h=h, par=par: e.dma_start(
                out=wg[par][:, :], in_=di["w_in"][:, h * 6144:(h + 1) * 6144]),
                after=[use_tok[par]])
            mmt = [None] * 6
            for blk in range(nblk):
                for tc2 in range(2):
                    psi = blk * 2 + tc2

                    def mmfn(e, par=par, blk=blk, tc2=tc2, psi=psi):
                        for kt in range(KT):
                            r = e.matmul(
                                PS[psi][:, :],
                                wg[par][:, kt * 384 + blk * 128:kt * 384 + (blk + 1) * 128],
                                xcbf[:, kt * TC + tc2 * 512:kt * TC + tc2 * 512 + 512],
                                start=(kt == 0), stop=(kt == KT - 1))
                        return r
                    mmt[psi] = p.task("tensor", mmfn,
                                      after=[wtok[par], ps_free[psi]])
            use_tok[par] = mmt[2 * nblk - 1]
            ev = []
            for tc2 in range(2):
                ev.append(p.task("scalar", lambda e, tc2=tc2, h=h, dtt=dtt: e.activation(
                    dtt[:, tc2 * 512:(tc2 + 1) * 512], PS[tc2][:, :],
                    AF.Identity, bias=dtb[:, h:h + 1]),
                    after=[mmt[tc2], V["vin"]]))
            ps_free[0] = ev[0]; ps_free[1] = ev[1]
            vev = []
            for tc2 in range(2):
                vev.append(p.task("scalar", lambda e, tc2=tc2, vv=vv: e.activation(
                    vv[:, tc2 * 512:(tc2 + 1) * 512], PS[2 + tc2][:, :],
                    AF.Identity, bias=kc[:, 4:5]),
                    after=[mmt[2 + tc2], V["gz_dma"], V["swp"], V.get("swp2"), V["r1"]]))
            ps_free[2] = vev[0]; ps_free[3] = vev[1]
            swp1 = p.task("sync", lambda e, vsw=vsw, vv=vv: e.dma_start(
                out=vsw[0:64, :], in_=vv[64:128, :]), after=[vev[1], V["cot"]])
            swp2 = p.task("sdma", lambda e, vsw=vsw, vv=vv: e.dma_start(
                out=vsw[64:128, :], in_=vv[0:64, :]), after=[vev[1], V["cot"]])
            r1 = p.task("gpsimd", lambda e, vr=vr, vv=vv: e.tensor_tensor(
                vr[:, :], vv[:, :], ct[:, :], ALU.mult), after=[vev[1], rfold, V["pr"]])
            if own:
                gz = []
                for tc2 in range(2):
                    gz.append(p.task("scalar", lambda e, tc2=tc2, vv=vv: e.activation(
                        vv[:, tc2 * 512:(tc2 + 1) * 512], PS[4 + tc2][:, :],
                        AF.Identity, bias=kc[:, 4:5]),
                        after=[mmt[4 + tc2], swp1, swp2, r1]))
                ps_free[4] = gz[0]; ps_free[5] = gz[1]
                V["gz_dma"] = p.task("sync", lambda e, h=h, vv=vv: e.dma_start(
                    out=scr_gz[:, h * TC:(h + 1) * TC], in_=vv[:, :]), after=[gz[1]])
            p.task("scalar", lambda e, tmpa=tmpa, dtt=dtt: e.activation(
                tmpa[:, :], dtt[:, :], AF.Abs),
                after=[V["el_dma"], V["pr"]])
            p.task("scalar", lambda e, tmpa=tmpa: e.activation(
                tmpa[:, :], tmpa[:, :], AF.Exp, scale=-1.0))
            lnt = p.task("scalar", lambda e, tmpa=tmpa: e.activation(
                tmpa[:, :], tmpa[:, :], AF.Ln, bias=kc[:, 2:3]))
            p.task("vector", lambda e, tmpb=tmpb, dtt=dtt: e.tensor_scalar(
                tmpb[:, :], dtt[:, :], 0.0, None, ALU.max), after=[ev[1]])
            p.task("vector", lambda e, dtt=dtt, tmpa=tmpa, tmpb=tmpb: e.tensor_tensor(
                dtt[:, :], tmpa[:, :], tmpb[:, :], ALU.add), after=[lnt])
            clt = p.task("vector", lambda e, dtt=dtt: e.tensor_scalar(
                dtt[:, :], dtt[:, :], 2.0, 0.001, ALU.min, ALU.max))
            la = p.task("vector", lambda e, tmpb=tmpb, dtt=dtt: e.tensor_scalar(
                tmpb[:, :], dtt[:, :], -1.0, None, ALU.mult))
            p.task("gpsimd", lambda e, vsw=vsw: e.tensor_tensor(
                vsw[:, :], vsw[:, :], stt[:, :], ALU.mult), after=[swp1, swp2])
            p.task("gpsimd", lambda e, vr=vr, vsw=vsw: e.tensor_tensor(
                vr[:, :], vr[:, :], vsw[:, :], ALU.add))
            vin = p.task("gpsimd", lambda e, vr=vr, dtt=dtt: e.tensor_tensor(
                vr[:, :], vr[:, :], dtt[:, :], ALU.mult), after=[clt])
            V["vin"] = vin
            V["swp"] = swp1
            V["swp2"] = swp2
            V["r1"] = r1
            for c in range(NCH):
                p.task("vector", lambda e, c=c, Lt=Lt, tmpb=tmpb: e.tensor_tensor_scan(
                    Lt[:, c * CH:(c + 1) * CH], tmpb[:, c * CH:(c + 1) * CH],
                    tmpb[:, c * CH:(c + 1) * CH], 0.0, ALU.add, ALU.bypass),
                    after=[la, V["el"]])
            p.task("vector", lambda e, Lt=Lt: e.tensor_scalar(
                Lt[:, :], Lt[:, :], 0.0, -20.0, ALU.min, ALU.max))
            bdt = bd_own[:, h * 4:(h + 1) * 4] if own else pbd
            bot = bo_own[:, h * 4:(h + 1) * 4] if own else pbo
            p.task("vector", lambda e, bdt=bdt, Lt=Lt: e.tensor_copy(bdt, Lt[:, 255:TC:256]), sw=True)
            p.task("vector", lambda e, l0=l0, Lt=Lt: e.tensor_copy(l0[:, :], Lt[:, 0:TC:256]), sw=True)
            nl0t = p.task("vector", lambda e, nl0=nl0, l0=l0: e.tensor_scalar(
                nl0[:, :], l0[:, :], -1.0, None, ALU.mult), sw=True)
            for c in range(NCH):
                p.task("scalar", lambda e, c=c, tmpa=tmpa, Lt=Lt, l0=l0: e.activation(
                    tmpa[:, c * CH:(c + 1) * CH], Lt[:, c * CH:(c + 1) * CH],
                    AF.Exp, bias=l0[:, c:c + 1], scale=-1.0), after=[nl0t])
                p.task("scalar", lambda e, c=c, eLs=eLs, Lt=Lt, nl0=nl0: e.activation(
                    eLs[:, c * CH:(c + 1) * CH], Lt[:, c * CH:(c + 1) * CH],
                    AF.Exp, bias=nl0[:, c:c + 1], scale=1.0))
            sct = p.now("scalar")
            pr = p.task("vector", lambda e, tmpa=tmpa, vr=vr: e.tensor_tensor(
                tmpa[:, :], tmpa[:, :], vr[:, :], ALU.mult), after=[sct, vin])
            V["pr"] = pr
            for c in range(NCH):
                p.task("vector", lambda e, c=c, vsw=vsw, tmpa=tmpa: e.tensor_tensor_scan(
                    vsw[:, c * CH:(c + 1) * CH], tmpa[:, c * CH:(c + 1) * CH],
                    tmpa[:, c * CH:(c + 1) * CH], 0.0, ALU.add, ALU.bypass))
            cum_tok = p.now("vector")
            cot = p.task("vector", lambda e, co=co, eLs=eLs, vsw=vsw: e.tensor_tensor(
                co[:, :], eLs[:, :], vsw[:, :], ALU.mult), after=[V["co_dma"]])
            V["cot"] = cot
            p.task("vector", lambda e, bot=bot, co=co: e.tensor_copy(bot, co[:, 255:TC:256]), sw=True)
            if own:
                el = p.task("scalar", lambda e, tmpa=tmpa, Lt=Lt: e.activation(
                    tmpa[:, :], Lt[:, :], AF.Exp), after=[cum_tok])
                V["el"] = el
                V["co_dma"] = p.task("sync", lambda e, h=h, co=co: e.dma_start(
                    out=scr_co[:, h * TC:(h + 1) * TC], in_=co[:, :]), after=[cot])
                V["el_dma"] = p.task("sync", lambda e, h=h, tmpa=tmpa: e.dma_start(
                    out=scr_el[:, h * TC:(h + 1) * TC], in_=tmpa[:, :]), after=[el])
            else:
                p.guard = True
                p.task("vector", lambda e, pc4=pc4, pbd=pbd: e.tensor_tensor_scan(
                    pc4[:, :], pbd, pbd, 0.0, ALU.add, ALU.bypass))
                p.task("vector", lambda e, e4=e4, pc4=pc4: e.tensor_scalar(
                    e4[:, :], pc4[:, :], 0.0, -80.0, ALU.min, ALU.max),
                    after=[V["el"]])
                p.task("vector", lambda e, e4=e4, pbd=pbd: e.tensor_tensor(
                    e4[:, :], e4[:, :], bcast(pbd[:, 0:1], 4), ALU.subtract))
                ncd = p.task("vector", lambda e, e4=e4: e.tensor_scalar(
                    e4[:, :], e4[:, :], 0.0, -20.0, ALU.min, ALU.max))
                ex = p.task("scalar", lambda e, en4=en4, e4=e4: e.activation(
                    en4[:, :], e4[:, :], AF.Exp, scale=-1.0), after=[ncd])
                V["el"] = ex
                p.task("vector", lambda e, sd4=sd4, pbo=pbo, en4=en4: e.tensor_tensor(
                    sd4[:, :], pbo, en4[:, :], ALU.mult), after=[ex])
                p.task("vector", lambda e, h=h, pbd=pbd: e.tensor_copy(
                    pstat[:, h * 3:h * 3 + 1], pbd[:, 0:1]))
                p.task("vector", lambda e, h=h, pc4=pc4: e.tensor_copy(
                    pstat[:, h * 3 + 1:h * 3 + 2], pc4[:, 3:4]))
                p.task("vector", lambda e, h=h, sd4=sd4: e.tensor_reduce(
                    pstat[:, h * 3 + 2:h * 3 + 3], sd4[:, :], AX, ALU.add))
        arf.pop()

    def phase_ffn(acc_cols):
        p.guard = False
        arf.push()
        esel = arf.alloc(TC, rows=8)
        lesel = p.task("sync", lambda e: e.dma_start(out=esel[:, :], in_=di["esel"][:, :]))
        cmb_fm = arf.alloc(TC, rows=8)
        zg = arf.alloc(TC, rows=1)
        cmb_tm = arf.alloc(64)
        sg_bc = arf.alloc(TC)
        cmb_bc = arf.alloc(TC)
        sil = arf.alloc(512)
        s8a = arf.alloc(8); s8b = arf.alloc(8); s8c = arf.alloc(8)
        sc1 = arf.alloc(4)
        hblk = xcbf[:, 0:FT * TC]
        psg = {i: None for i in range(6)}

        lg = lg_s
        lg_done = None
        sg_w = spare_bf[:, 1164:1180]
        lsg = p.task("gdma", lambda e: e.dma_start(out=sg_w[:, :], in_=di["w_sg"][:, :]))
        for tc2 in range(2):
            def mms(e, tc2=tc2):
                for kt in range(KT):
                    r = e.matmul(PS[1][0:1, :], sg_w[:, kt:kt + 1],
                                 xnbf[:, kt * TC + tc2 * 512:kt * TC + tc2 * 512 + 512],
                                 start=(kt == 0), stop=(kt == KT - 1))
                return r
            mm = p.task("tensor", mms, after=[lsg, psg[1]])
            psg[1] = p.task("vector", lambda e, tc2=tc2: e.tensor_copy(
                zg[:, tc2 * 512:(tc2 + 1) * 512], PS[1][0:1, :]), after=[mm])
        sgm = p.task("scalar", lambda e: e.activation(zg[:, :], zg[:, :], AF.Sigmoid),
                     after=[psg[1]])
        for tc2 in range(2):
            mm = p.task("tensor", lambda e, tc2=tc2: e.matmul(
                PS[2][:, :], ones_f[:, :], zg[0:1, tc2 * 512:(tc2 + 1) * 512],
                start=True, stop=True), after=[sgm, psg[2]])
            psg[2] = p.task("vector", lambda e, tc2=tc2: e.tensor_copy(
                sg_bc[:, tc2 * 512:(tc2 + 1) * 512], PS[2][:, :]), after=[mm])
        sg_done = psg[2]
        p.guard = True
        for gi in range(8):
            tr = p.task("tensor", lambda e, gi=gi: e.transpose(
                PS[3][:, 0:8], lg[:, gi * 128:(gi + 1) * 128], ident[0:8, 0:8]),
                after=[lg_done, psg[3]])
            psg[3] = p.task("vector", lambda e: e.tensor_copy(s8a[:, :], PS[3][:, 0:8]),
                            after=[tr])
            p.task("vector", lambda e: e.tensor_reduce(sc1[:, 0:1], s8a[:, :], AX, ALU.max))
            p.task("vector", lambda e: e.tensor_scalar(
                sc1[:, 1:2], sc1[:, 0:1], -1.0, None, ALU.mult))
            vt = p.now("vector")
            et = p.task("scalar", lambda e: e.activation(
                s8b[:, :], s8a[:, :], AF.Exp, bias=sc1[:, 1:2]), after=[vt])
            p.task("vector", lambda e: e.tensor_reduce(
                sc1[:, 0:1], s8b[:, :], AX, ALU.add), after=[et])
            p.task("vector", lambda e: e.reciprocal(sc1[:, 1:2], sc1[:, 0:1]))
            p.task("vector", lambda e: e.tensor_tensor(
                s8a[:, :], s8b[:, :], bcast(sc1[:, 1:2], 8), ALU.mult))
            p.task("vector", lambda e: e.tensor_reduce(
                sc1[:, 0:1], s8a[:, :], AX, ALU.max))
            p.task("vector", lambda e: e.tensor_tensor(
                s8b[:, :], s8a[:, :], bcast(sc1[:, 0:1], 8), ALU.subtract))
            p.task("vector", lambda e: e.scalar_tensor_tensor(
                s8b[:, :], s8b[:, :], -BIG, iota[:, :], ALU.mult, ALU.add))
            p.task("vector", lambda e: e.tensor_reduce(
                sc1[:, 1:2], s8b[:, :], AX, ALU.min))
            p.task("vector", lambda e: e.tensor_tensor(
                s8b[:, :], iota[:, :], bcast(sc1[:, 1:2], 8), ALU.subtract))
            p.task("vector", lambda e: e.tensor_tensor(
                s8b[:, :], s8b[:, :], s8b[:, :], ALU.mult))
            p.task("vector", lambda e: e.tensor_scalar(
                s8b[:, :], s8b[:, :], -1.0, 1.0, ALU.mult, ALU.add))
            p.task("vector", lambda e: e.tensor_scalar(
                s8b[:, :], s8b[:, :], 0.0, None, ALU.max))
            p.task("vector", lambda e: e.scalar_tensor_tensor(
                s8c[:, :], s8b[:, :], -BIG, s8a[:, :], ALU.mult, ALU.add))
            p.task("vector", lambda e: e.tensor_reduce(
                sc1[:, 2:3], s8c[:, :], AX, ALU.max))
            p.task("vector", lambda e: e.tensor_tensor(
                s8c[:, :], s8c[:, :], bcast(sc1[:, 2:3], 8), ALU.subtract))
            p.task("vector", lambda e: e.scalar_tensor_tensor(
                s8c[:, :], s8c[:, :], -BIG, iota[:, :], ALU.mult, ALU.add))
            p.task("vector", lambda e: e.tensor_reduce(
                sc1[:, 3:4], s8c[:, :], AX, ALU.min))
            p.task("vector", lambda e: e.tensor_tensor(
                s8c[:, :], iota[:, :], bcast(sc1[:, 3:4], 8), ALU.subtract))
            p.task("vector", lambda e: e.tensor_tensor(
                s8c[:, :], s8c[:, :], s8c[:, :], ALU.mult))
            p.task("vector", lambda e: e.tensor_scalar(
                s8c[:, :], s8c[:, :], -1.0, 1.0, ALU.mult, ALU.add))
            p.task("vector", lambda e: e.tensor_scalar(
                s8c[:, :], s8c[:, :], 0.0, None, ALU.max))
            p.task("vector", lambda e: e.tensor_tensor(
                s8b[:, :], s8b[:, :], bcast(sc1[:, 0:1], 8), ALU.mult))
            p.task("vector", lambda e, gi=gi: e.scalar_tensor_tensor(
                cmb_tm[:, gi * 8:(gi + 1) * 8], s8c[:, :], sc1[:, 2:3], s8b[:, :],
                ALU.mult, ALU.add))
        cmb_done = p.now("vector")
        if tap:
            p.task("sync", lambda e: e.dma_start(out=tp["t_cmb"][:, :], in_=cmb_tm[:, :]),
                   after=[cmb_done])
        for gi in range(8):
            tr = p.task("tensor", lambda e, gi=gi: e.transpose(
                PS[3][0:8, 0:128], cmb_tm[:, gi * 8:(gi + 1) * 8], ident[:, :]),
                after=[cmb_done, psg[3]])
            psg[3] = p.task("vector", lambda e, gi=gi: e.tensor_copy(
                cmb_fm[:, gi * 128:(gi + 1) * 128], PS[3][0:8, 0:128]), after=[tr])
        cmb_fm_done = psg[3]
        p.guard = False

        wtok = [None, None]
        wuse = [None, None]
        dtok = [None, None]
        duse = [None, None]
        huse = [None]
        acc_tok = [None] * 32

        def gu_block(wname, base_col, scale_cmb, cmb_tok):
            hw = []
            for ft in range(FT):
                par = ft % 2
                wtok[par] = p.task("gdma", lambda e, par=par, wname=wname, base_col=base_col, ft=ft: e.dma_start(
                    out=wg[par][:, 0:4096],
                    in_=di[wname][:, base_col + ft * 4096:base_col + (ft + 1) * 4096]),
                    after=[wuse[par]])
                for tc2 in range(2):
                    def mmg(e, par=par, tc2=tc2):
                        for kt in range(KT):
                            r = e.matmul(PS[0 + tc2][:, :],
                                         wg[par][:, kt * 128:(kt + 1) * 128],
                                         xnbf[:, kt * TC + tc2 * 512:kt * TC + tc2 * 512 + 512],
                                         start=(kt == 0), stop=(kt == KT - 1))
                        return r

                    def mmu(e, par=par, tc2=tc2):
                        for kt in range(KT):
                            r = e.matmul(PS[2 + tc2][:, :],
                                         wg[par][:, 2048 + kt * 128:2048 + (kt + 1) * 128],
                                         xnbf[:, kt * TC + tc2 * 512:kt * TC + tc2 * 512 + 512],
                                         start=(kt == 0), stop=(kt == KT - 1))
                        return r
                    mg = p.task("tensor", mmg, after=[wtok[par], psg[0 + tc2]])
                    mu = p.task("tensor", mmu, after=[wtok[par], psg[2 + tc2]])
                    sl = p.task("scalar", lambda e, tc2=tc2: e.activation(
                        sil[:, :], PS[0 + tc2][:, :], AF.Silu), after=[mg])
                    psg[0 + tc2] = sl
                    if scale_cmb is None:
                        hv = p.task("vector", lambda e, ft=ft, tc2=tc2: e.tensor_tensor(
                            hblk[:, ft * TC + tc2 * 512:ft * TC + tc2 * 512 + 512],
                            sil[:, :], PS[2 + tc2][:, :], ALU.mult),
                            after=[sl, mu, huse[0]])
                    else:
                        hv0 = p.task("vector", lambda e, tc2=tc2: e.tensor_tensor(
                            sil[:, :], sil[:, :], PS[2 + tc2][:, :], ALU.mult),
                            after=[sl, mu])
                        hv = p.task("vector", lambda e, ft=ft, tc2=tc2: e.tensor_tensor(
                            hblk[:, ft * TC + tc2 * 512:ft * TC + tc2 * 512 + 512],
                            sil[:, :], scale_cmb[:, tc2 * 512:(tc2 + 1) * 512], ALU.mult),
                            after=[hv0, huse[0], cmb_tok])
                    psg[2 + tc2] = hv
                    wuse[par] = hv
                    hw.append(hv)
            return hw

        def dn_block(wname, base_col, first, hw):
            last = []
            for fo in range(16):
                par = fo % 2
                dtok[par] = p.task("gdma", lambda e, par=par, wname=wname, base_col=base_col, fo=fo: e.dma_start(
                    out=wd[par][:, :],
                    in_=di[wname][:, base_col + fo * 1408:base_col + (fo + 1) * 1408]),
                    after=[duse[par]])
                for tc2 in range(2):
                    def mmd(e, par=par, tc2=tc2):
                        for kt in range(FT):
                            r = e.matmul(PS[4 + tc2][:, :],
                                         wd[par][:, kt * 128:(kt + 1) * 128],
                                         hblk[:, kt * TC + tc2 * 512:kt * TC + tc2 * 512 + 512],
                                         start=(kt == 0), stop=(kt == FT - 1))
                        return r
                    md = p.task("tensor", mmd,
                                after=[dtok[par], hw[-1], acc_tok[fo * 2 + tc2], psg[4 + tc2]])
                    asl = acc_cols[:, fo * TC + tc2 * 512:fo * TC + tc2 * 512 + 512]
                    if first:
                        at = p.task("vector", lambda e, asl=asl, tc2=tc2: e.tensor_copy(
                            asl, PS[4 + tc2][:, :]), after=[md])
                    else:
                        at = p.task("vector", lambda e, asl=asl, tc2=tc2: e.tensor_tensor(
                            asl, PS[4 + tc2][:, :], asl, ALU.add), after=[md])
                    psg[4 + tc2] = at
                    acc_tok[fo * 2 + tc2] = at
                    duse[par] = at
                    last.append(at)
            return last

        for b in range(SHB):
            hw = gu_block("w_shgu", b * FT * 4096, None, None)
            dl = dn_block("w_shdn", b * 16 * 1408, b == 0, hw)
            huse[0] = dl[-1]
        shtap = None
        if tap:
            shtap = p.task("sync", lambda e: e.dma_start(
                out=tp["t_shonly"][:, :], in_=acc_cols[:, :]), after=[acc_tok[-1]])
        for fo in range(16):
            for tc2 in range(2):
                asl = acc_cols[:, fo * TC + tc2 * 512:fo * TC + tc2 * 512 + 512]
                t = p.task("vector", lambda e, asl=asl, tc2=tc2: e.tensor_tensor(
                    asl, asl, sg_bc[:, tc2 * 512:(tc2 + 1) * 512], ALU.mult),
                    after=[sg_done, acc_tok[fo * 2 + tc2], shtap])
                acc_tok[fo * 2 + tc2] = t
        for ei in range(NE):
            cbt = None
            for tc2 in range(2):
                mm = p.task("tensor", lambda e, ei=ei, tc2=tc2: e.matmul(
                    PS[4][:, :], esel[:, ei * 128:(ei + 1) * 128],
                    cmb_fm[0:8, tc2 * 512:(tc2 + 1) * 512],
                    start=True, stop=True),
                    after=[cmb_fm_done, lesel, psg[4], duse[0], duse[1]])
                psg[4] = p.task("vector", lambda e, tc2=tc2: e.tensor_copy(
                    cmb_bc[:, tc2 * 512:(tc2 + 1) * 512], PS[4][:, :]), after=[mm])
            cbt = psg[4]
            hw = gu_block("w_egu", ei * FT * 4096, cmb_bc, cbt)
            hw.append(cbt)
            dl = dn_block("w_edn", ei * 16 * 1408, False, hw)
            huse[0] = dl[-1]
        gt = []
        for fo in range(16):
            gt.append(p.task("scalar", lambda e, fo=fo: e.activation(
                acc_cols[:, fo * TC:(fo + 1) * TC], acc_cols[:, fo * TC:(fo + 1) * TC],
                AF.Silu), after=[acc_tok[fo * 2], acc_tok[fo * 2 + 1]]))
        if tap:
            p.task("sync", lambda e: e.dma_start(out=tp["t_acc"][:, :], in_=acc_cols[:, :]),
                   after=[gt[-1]])
        arf.pop()
        p.guard = True

    def phase_tail(acc_cols):
        p.guard = False
        arf.push()
        mb_t = arf.alloc(TC); rb_t = arf.alloc(TC)
        sm = arf.alloc(48)
        stab = sm[:, 0:1]; pb = sm[:, 1:2]; sb0 = sm[:, 2:3]
        pc4 = sm[:, 4:8]; cd4 = sm[:, 8:12]; E4 = sm[:, 12:16]
        En4 = sm[:, 16:20]; sd4 = sm[:, 20:24]; si4 = sm[:, 24:28]
        ca4 = sm[:, 28:32]
        fin = xcbf
        vmg = xnbf
        x3 = di["xt"].rearrange("p (kt t) -> p kt t", kt=KT)

        bmt = []
        for tc2 in range(2):
            for psi, row, dst in [(0, m_row, mb_t), (1, r_row, rb_t)]:
                mm = p.task("tensor", lambda e, psi=psi, row=row, tc2=tc2: e.matmul(
                    PS[psi][:, :], ones_f[:, :], row[0:1, 3 + tc2 * 512:3 + tc2 * 512 + 512],
                    start=True, stop=True), after=bmt[-1:])
                bmt.append(p.task("vector", lambda e, psi=psi, dst=dst, tc2=tc2: e.tensor_copy(
                    dst[:, tc2 * 512:(tc2 + 1) * 512], PS[psi][:, :]), after=[mm]))
        bdone = bmt[-1]

        arf.push()
        co_t = [arf.alloc(TC), arf.alloc(TC)]
        el_t = [arf.alloc(TC), arf.alloc(TC)]
        stab16 = arf.alloc(16); pb16 = arf.alloc(16); sb16 = arf.alloc(16)
        tmp16 = arf.alloc(16)
        pc64 = arf.alloc(64); cd64 = arf.alloc(64); E64 = arf.alloc(64)
        En64 = arf.alloc(64); sd64 = arf.alloc(64); si64 = arf.alloc(64)
        sh64 = arf.alloc(64); ca64 = arf.alloc(64)

        def v3(ap, b):
            return ap.rearrange("p (a b) -> p a b", b=b)

        def bc3(col16, b):
            return col16.rearrange("p (a b) -> p a b", b=1).to_broadcast([128, 16, b])

        p.guard = True
        p.task("vector", lambda e: e.tensor_tensor(
            stab16[:, :], pstat[:, 0:48:3], bcast(mflag[:, 0:1], 16), ALU.mult))
        p.task("vector", lambda e: e.tensor_tensor(
            tmp16[:, :], bd_own[:, 0:64:4], bcast(m1f[:, 0:1], 16), ALU.mult))
        p.task("vector", lambda e: e.tensor_tensor(
            stab16[:, :], stab16[:, :], tmp16[:, :], ALU.add))
        p.task("vector", lambda e: e.tensor_tensor(
            pb16[:, :], pstat[:, 1:48:3], bcast(mflag[:, 0:1], 16), ALU.mult))
        p.task("vector", lambda e: e.tensor_tensor(
            sb16[:, :], pstat[:, 2:48:3], bcast(mflag[:, 0:1], 16), ALU.mult))
        for h in range(NH):
            p.task("vector", lambda e, h=h: e.tensor_tensor_scan(
                pc64[:, 4 * h:4 * h + 4], bd_own[:, 4 * h:4 * h + 4],
                bd_own[:, 4 * h:4 * h + 4], 0.0, ALU.add, ALU.bypass))
        p.task("vector", lambda e: e.tensor_tensor(
            v3(pc64, 4), v3(pc64, 4), bc3(pb16, 4), ALU.add))
        p.task("vector", lambda e: e.tensor_scalar(
            cd64[:, :], pc64[:, :], 0.0, -80.0, ALU.min, ALU.max))
        p.task("vector", lambda e: e.tensor_tensor(
            v3(cd64, 4), v3(cd64, 4), bc3(stab16, 4), ALU.subtract))
        nct = p.task("vector", lambda e: e.tensor_scalar(
            cd64[:, :], cd64[:, :], 0.0, -20.0, ALU.min, ALU.max))
        p.task("scalar", lambda e: e.activation(E64[:, :], cd64[:, :], AF.Exp),
               after=[nct])
        ext = p.task("scalar", lambda e: e.activation(
            En64[:, :], cd64[:, :], AF.Exp, scale=-1.0))
        p.task("vector", lambda e: e.tensor_tensor(
            sd64[:, :], bo_own[:, :], En64[:, :], ALU.mult), after=[ext])
        for h in range(NH):
            p.task("vector", lambda e, h=h: e.tensor_tensor_scan(
                si64[:, 4 * h:4 * h + 4], sd64[:, 4 * h:4 * h + 4],
                sd64[:, 4 * h:4 * h + 4], 0.0, ALU.add, ALU.bypass))
        p.task("vector", lambda e: e.tensor_copy(
            v3(sh64, 4)[:, :, 1:4], v3(si64, 4)[:, :, 0:3]))
        p.task("vector", lambda e: e.tensor_copy(sh64[:, 0:64:4], sb16[:, :]))
        p.task("vector", lambda e: e.tensor_tensor(
            v3(sh64, 4)[:, :, 1:4], v3(sh64, 4)[:, :, 1:4], bc3(sb16, 3), ALU.add))
        cat = p.task("vector", lambda e: e.tensor_tensor(
            ca64[:, :], sh64[:, :], E64[:, :], ALU.mult))
        p.guard = False
        fin_tok = []
        lco = [None, None]
        lel = [None, None]
        for h in range(NH):
            hp = h % 2
            lco[hp] = p.task("sync", lambda e, h=h, hp=hp: e.dma_start(
                out=co_t[hp][:, :], in_=scr_co[:, h * TC:(h + 1) * TC]),
                after=fin_tok[-2:-1] if len(fin_tok) >= 2 else [])
            lel[hp] = p.task("sync", lambda e, h=h, hp=hp: e.dma_start(
                out=el_t[hp][:, :], in_=scr_el[:, h * TC:(h + 1) * TC]))
            p.task("vector", lambda e, h=h, hp=hp: e.tensor_tensor(
                v3(el_t[hp], CH)[:, :, :].opt(),
                el_t[hp][:, :].rearrange("p (a b) -> p a b", b=CH),
                ca64[:, 4 * h:4 * h + 4].rearrange("p (a b) -> p a b", b=1)
                .to_broadcast([128, 4, CH]), ALU.mult), after=[lel[hp], cat])
            ft = p.task("vector", lambda e, h=h, hp=hp: e.tensor_tensor(
                fin[:, h * TC:(h + 1) * TC], el_t[hp][:, :], co_t[hp][:, :], ALU.add),
                after=[lco[hp]])
            fin_tok.append(ft)
        arf.pop()
        if tap:
            p.task("sync", lambda e: e.dma_start(out=tp["t_fin"][:, :], in_=fin[:, :]),
                   after=fin_tok[-1:])

        arf.push()
        gz_t = [arf.alloc(TC), arf.alloc(TC)]
        wtok = [None, None]
        wuse = [None, None]
        vmg_tok = [None]
        lgz = [fin_tok[-1], fin_tok[-1]]
        sg = [None, None]
        for fo in range(16):
            par = fo % 2
            wtok[par] = p.task("gdma", lambda e, par=par, fo=fo: e.dma_start(
                out=wg[par][:, 0:2048], in_=di["w_mb"][:, fo * 2048:(fo + 1) * 2048]),
                after=[wuse[par]])
            lgz[par] = p.task("sync", lambda e, fo=fo, par=par: e.dma_start(
                out=gz_t[par][:, :], in_=scr_gz[:, fo * TC:(fo + 1) * TC]),
                after=[lgz[par], vmg_tok[-2] if len(vmg_tok) >= 2 else None])
            sg[par] = p.task("scalar", lambda e, par=par: e.activation(
                gz_t[par][:, :], gz_t[par][:, :], AF.Sigmoid), after=[lgz[par]])
            for tc2 in range(2):
                def mmv(e, par=par, tc2=tc2):
                    for kt in range(KT):
                        r = e.matmul(PS[2 + tc2][:, :],
                                     wg[par][:, kt * 128:(kt + 1) * 128],
                                     fin[:, kt * TC + tc2 * 512:kt * TC + tc2 * 512 + 512],
                                     start=(kt == 0), stop=(kt == KT - 1))
                    return r
                mm = p.task("tensor", mmv,
                            after=[wtok[par], fin_tok[-1], vmg_tok[-2] if len(vmg_tok) >= 2 else None])
                vt = p.task("vector", lambda e, fo=fo, tc2=tc2, par=par: e.tensor_tensor(
                    vmg[:, fo * TC + tc2 * 512:fo * TC + tc2 * 512 + 512],
                    PS[2 + tc2][:, :], gz_t[par][:, tc2 * 512:(tc2 + 1) * 512], ALU.mult),
                    after=[mm, sg[par]])
                vmg_tok.append(vt)
                wuse[par] = vt
        vmg_done = vmg_tok[-1]
        arf.pop()
        if tap:
            p.task("sync", lambda e: e.dma_start(out=tp["t_vmg"][:, :], in_=vmg[:, :]),
                   after=[vmg_done])

        arf.push()
        x_t = [arf.alloc(TC), arf.alloc(TC)]
        xn_t = [arf.alloc(TC), arf.alloc(TC)]
        wuse = [None, None]
        lx = [vmg_done, vmg_done]
        out_done = [None, None]
        ydma = [None, None]
        for fo in range(16):
            par = fo % 2
            wtok[par] = p.task("gdma", lambda e, par=par, fo=fo: e.dma_start(
                out=wd[par][:, 0:1408], in_=di["w_out"][:, fo * 2048:fo * 2048 + 1408]),
                after=[wuse[par]])
            wtok2 = p.task("gdma", lambda e, par=par, fo=fo: e.dma_start(
                out=wg[par][:, 0:640], in_=di["w_out"][:, fo * 2048 + 1408:(fo + 1) * 2048]))
            lx[par] = p.task("sync", lambda e, fo=fo, par=par: e.dma_start(
                out=x_t[par][:, :], in_=x3[:, fo, 3:XC]), after=[lx[par], wuse[par]])
            x1 = p.task("gpsimd", lambda e, par=par: e.tensor_tensor(
                xn_t[par][:, :], x_t[par][:, :], mb_t[:, :], ALU.subtract),
                after=[lx[par], bdone, ydma[par]])
            x2 = p.task("gpsimd", lambda e, par=par: e.tensor_tensor(
                xn_t[par][:, :], xn_t[par][:, :], rb_t[:, :], ALU.mult))
            for tc2 in range(2):
                def mmo(e, par=par, tc2=tc2):
                    for kt in range(KT):
                        w = wd[par][:, kt * 128:(kt + 1) * 128] if kt < FT else \
                            wg[par][:, (kt - FT) * 128:(kt - FT + 1) * 128]
                        r = e.matmul(PS[2 * par + tc2][:, :], w,
                                     vmg[:, kt * TC + tc2 * 512:kt * TC + tc2 * 512 + 512],
                                     start=(kt == 0), stop=(kt == KT - 1))
                    return r
                mm = p.task("tensor", mmo,
                            after=[wtok[par], wtok2, vmg_done, out_done[-4] if len(out_done) >= 4 else None])
                p.task("vector", lambda e, tc2=tc2, par=par: e.tensor_tensor(
                    xn_t[par][:, tc2 * 512:(tc2 + 1) * 512], PS[2 * par + tc2][:, :],
                    xn_t[par][:, tc2 * 512:(tc2 + 1) * 512], ALU.add), after=[mm, x2])
                p.task("vector", lambda e, fo=fo, tc2=tc2, par=par: e.tensor_tensor(
                    xn_t[par][:, tc2 * 512:(tc2 + 1) * 512],
                    xn_t[par][:, tc2 * 512:(tc2 + 1) * 512],
                    acc_cols[:, fo * TC + tc2 * 512:fo * TC + tc2 * 512 + 512], ALU.mult))
                s3 = p.task("vector", lambda e, tc2=tc2, par=par: e.tensor_tensor(
                    xn_t[par][:, tc2 * 512:(tc2 + 1) * 512],
                    xn_t[par][:, tc2 * 512:(tc2 + 1) * 512],
                    x_t[par][:, tc2 * 512:(tc2 + 1) * 512], ALU.add))
                out_done.append(s3)
            wuse[par] = out_done[-1]
            ydma[par] = p.task("sync", lambda e, fo=fo, par=par: e.dma_start(
                out=y_d[:, fo * TC:(fo + 1) * TC], in_=xn_t[par][:, :]), after=[out_done[-1]])
        arf.pop()
        arf.pop()
        p.guard = True

    phase_p0("xp", own=False)
    p.barrier()
    phase_front(own=False)
    p.barrier()
    phase_p0("xt", own=True)
    p.barrier()
    if tap:
        p.task("sync", lambda e: e.dma_start(out=tp["t_xnbf"][:, :], in_=xnbf[:, :]))
        p.task("sync", lambda e: e.dma_start(out=tp["t_xc"][:, :], in_=xcbf[:, :]))
        p.task("sync", lambda e: e.dma_start(out=tp["t_rows"][:, 0:XC], in_=m_row[:, :]))
        p.task("sync", lambda e: e.dma_start(out=tp["t_rows"][:, XC:2 * XC], in_=r_row[:, :]))
    phase_front(own=True)
    p.barrier()
    if tap:
        p.task("sync", lambda e: e.dma_start(out=tp["t_bdbo"][:, 0:64], in_=bd_own[:, :]))
        p.task("sync", lambda e: e.dma_start(out=tp["t_bdbo"][:, 64:128], in_=bo_own[:, :]))
        p.task("sync", lambda e: e.dma_start(out=tp["t_bdbo"][:, 128:176], in_=pstat[:, :]))
    arf.pop()
    acc_cols = arf.alloc(KT * TC)
    phase_ffn(acc_cols)
    p.barrier()
    phase_tail(acc_cols)


def _tile_k(w):
    c = w.shape[1]
    return np.ascontiguousarray(
        w.reshape(16, 128, c).transpose(1, 0, 2).reshape(128, 16 * c))


def _host_prep(inputs):
    f = np.float32
    x = np.asarray(inputs["L0_attn_x"], f)
    w_in = np.ascontiguousarray(
        np.asarray(inputs["L0_attn_in_proj_w"], f).T
        .reshape(16, 128, 3, 16, 128).transpose(1, 3, 0, 2, 4).reshape(128, -1))
    w_rt = _tile_k(np.ascontiguousarray(np.asarray(inputs["L0_ffn_W_router"], f).T))
    w_sg = _tile_k(np.ascontiguousarray(np.asarray(inputs["L0_ffn_W_sgate"], f).T))
    G = np.asarray(inputs["L0_ffn_Ws_gate_up"], f)
    ga = G[:, :5632].reshape(2048, SHB, FT, 128)
    up = G[:, 5632:].reshape(2048, SHB, FT, 128)
    st = np.stack([ga, up], axis=2)
    w_shgu = np.ascontiguousarray(
        st.reshape(16, 128, SHB, 2, FT, 128).transpose(1, 2, 4, 3, 0, 5).reshape(128, -1))
    Dn = np.asarray(inputs["L0_ffn_Ws_down"], f).reshape(SHB, FT, 128, 16, 128)
    w_shdn = np.ascontiguousarray(Dn.transpose(2, 0, 3, 1, 4).reshape(128, -1))
    Ge = np.asarray(inputs["L0_ffn_We_gate_up"], f)
    eg = Ge[:, :, :1408].reshape(NE, 16, 128, FT, 128)
    eu = Ge[:, :, 1408:].reshape(NE, 16, 128, FT, 128)
    est = np.stack([eg, eu], axis=3)
    w_egu = np.ascontiguousarray(est.transpose(2, 0, 4, 3, 1, 5).reshape(128, -1))
    De = np.asarray(inputs["L0_ffn_We_down"], f).reshape(NE, FT, 128, 16, 128)
    w_edn = np.ascontiguousarray(De.transpose(2, 0, 3, 1, 4).reshape(128, -1))
    hm = np.asarray(inputs["L0_attn_head_mix"], f)
    mb = np.zeros((16, 128, 16, 128), f)
    for i in range(128):
        mb[:, i, :, i] = hm
    w_mb = np.ascontiguousarray(
        mb.reshape(2048, 2048).reshape(16, 128, 16, 128).transpose(1, 2, 0, 3).reshape(128, -1))
    w_out = np.ascontiguousarray(
        np.asarray(inputs["L0_attn_out_proj_w"], f).T
        .reshape(16, 128, 16, 128).transpose(1, 2, 0, 3).reshape(128, -1))
    cwv = np.ascontiguousarray(
        np.asarray(inputs["L0_attn_conv_w"], f).reshape(16, 128, 4).transpose(1, 0, 2).reshape(128, 64))
    cbv = np.ascontiguousarray(np.asarray(inputs["L0_attn_conv_b"], f).reshape(16, 128).T)
    nscv = np.ascontiguousarray(np.asarray(inputs["L0_attn_norm_scale"], f).reshape(16, 128).T)
    dtbv = np.ascontiguousarray(np.asarray(inputs["L0_attn_dt_bias"], f).reshape(16, 128).T)
    iota8 = np.broadcast_to(np.arange(8, dtype=f), (128, 8)).copy()
    esel = np.zeros((8, 1024), f)
    for e_ in range(8):
        esel[e_, e_ * 128:(e_ + 1) * 128] = 1.0
    ident = np.eye(128, dtype=f)

    def rope_tables(t0):
        t = np.arange(t0, t0 + TC, dtype=np.float64)
        inv = (1.0 / 10000.0) ** (np.arange(0, 128, 2, dtype=np.float64) / 128.0)
        fr = np.outer(inv, t)
        cos = np.cos(fr); sin = np.sin(fr)
        CT = np.concatenate([cos, cos], 0)
        ST = np.concatenate([-sin, sin], 0)
        out = []
        for tab in (CT, ST):
            hi = tab.astype(f)
            lo = (tab - hi.astype(np.float64)).astype(f)
            out.append((hi, lo))
        return out

    shared = dict(w_in=w_in, w_rt=w_rt, w_sg=w_sg, w_shgu=w_shgu, w_shdn=w_shdn,
                  w_egu=w_egu, w_edn=w_edn, w_mb=w_mb, w_out=w_out, cw=cwv,
                  cb=cbv, nsc=nscv, dtb=dtbv, iota8=iota8, ident=ident, esel=esel)

    def xfm(xseq, halo):
        xa = np.concatenate([halo, xseq], 0)
        return np.ascontiguousarray(
            xa.T.reshape(16, 128, XC).transpose(1, 0, 2).reshape(128, -1))

    rt = [rope_tables(0), rope_tables(1024)]
    in_maps = []
    for c in range(8):
        b, hh = c // 2, c % 2
        start = hh * 1024
        halo = x[b, start - 3:start] if hh == 1 else np.zeros((3, 2048), f)
        xt = xfm(x[b, start:start + 1024], halo)
        if hh == 1:
            xp = xfm(x[b, 0:1024], np.zeros((3, 2048), f))
        else:
            xp = np.zeros((128, KT * XC), f)
        (cth, ctl), (sth, stl) = rt[hh]
        (pch, pcl), (psh, psl) = rt[0]
        rope = np.concatenate([cth, sth, pch, psh, ctl, stl, pcl, psl], 1)
        m = np.full((128, 1), 1.0 if hh == 1 else 0.0, f)
        im = dict(shared)
        im.update(xt=xt, xp=xp, rope=np.ascontiguousarray(rope), mflag=m)
        in_maps.append(im)
    return in_maps


_NC_CACHE = {}


def kernel(**inputs):
    if "nc" not in _NC_CACHE:
        _NC_CACHE["nc"] = build_nc(tap=False)
    nc = _NC_CACHE["nc"]
    in_maps = _host_prep(inputs)
    res = run_bass_kernel_spmd(nc, in_maps, core_ids=list(range(8)))
    out = np.empty((4, 2048, 2048), np.float32)
    for c in range(8):
        b, hh = c // 2, c % 2
        y = np.asarray(res.results[c]["y"]).reshape(128, 16, TC)
        out[b, hh * 1024:(hh + 1) * 1024] = y.transpose(2, 1, 0).reshape(TC, 2048)
    return out


# version 09
# speedup vs baseline: 1.0595x; 1.0595x over previous
import contextlib
import numpy as np

import concourse.bass as bass
import concourse.mybir as mybir
from concourse.bass_utils import run_bass_kernel_spmd

F32 = mybir.dt.float32
BF16 = mybir.dt.bfloat16
AF = mybir.ActivationFunctionType
ALU = mybir.AluOpType
AX = mybir.AxisListType.X

D = 2048
NH = 16
KT = 16
TC = 1024
CH = 256
NCH = 4
XC = 1027
NE = 8
SHB = 4
FT = 11
BIG = 1.0e9

NLANES = 6
CH_ENGINE = {
    "sync": "sync", "tensor": "tensor", "vector": "vector",
    "scalar": "scalar", "gpsimd": "gpsimd", "gdma": "gpsimd", "sdma": "scalar",
}
DMA_CH = ("sync", "gdma", "sdma")
for _c in DMA_CH:
    for _l in range(NLANES):
        CH_ENGINE[f"{_c}{_l}"] = CH_ENGINE[_c]
LANE_CH = tuple(f"{c}{l}" for c in DMA_CH for l in range(NLANES))


class Prog:

    SELF_WAIT = True

    def __init__(self):
        self.eng_tasks = {e: [] for e in ("sync", "tensor", "vector", "scalar", "gpsimd")}
        self.cnt = {c: 0 for c in CH_ENGINE}

    guard = True

    def task(self, ch, fn, after=(), sw=None):
        if sw is None:
            sw = self.guard
        deps = []
        for t in after:
            if t is None:
                continue
            if isinstance(t, list):
                deps.extend(x for x in t if x is not None)
            else:
                deps.append(t)
        if ch in DMA_CH:
            n = self.cnt[ch]
            lane = n % NLANES
            lch = f"{ch}{lane}"
            gen = self.cnt[lch] // 16
            if gen > 0:
                deps.append((lch, 16 * gen))
            self.cnt[ch] = n + 1
            self.cnt[lch] += 16
            self.eng_tasks[CH_ENGINE[ch]].append((lch, fn, deps, 16, sw))
            return (lch, self.cnt[lch])
        self.cnt[ch] += 1
        self.eng_tasks[CH_ENGINE[ch]].append((ch, fn, deps, 1, sw))
        return (ch, self.cnt[ch])

    def now(self, ch):
        if ch in DMA_CH:
            return [(f"{ch}{l}", self.cnt[f"{ch}{l}"])
                    for l in range(NLANES) if self.cnt[f"{ch}{l}"] > 0]
        return (ch, self.cnt[ch])

    def frontier(self):
        toks = []
        for c in list(DMA_CH) + ["tensor", "vector", "scalar", "gpsimd"]:
            t = self.now(c)
            toks.extend(t if isinstance(t, list) else ([t] if t[1] > 0 else []))
        return toks

    def barrier(self, exempt=("gdma",)):
        toks = []
        for c in list(DMA_CH) + ["tensor", "vector", "scalar", "gpsimd"]:
            if c in exempt:
                continue
            t = self.now(c)
            toks.extend(t if isinstance(t, list) else ([t] if t[1] > 0 else []))
        for eng in self.eng_tasks:
            self.eng_tasks[eng].append((None, None, list(toks), 0, True))

    def build(self, block, sems):
        for eng_name, tasks in self.eng_tasks.items():
            if not tasks:
                continue

            def mk(eng_name, tasks):
                def emit(eng):
                    seen = {}
                    cnt = {}
                    for ch, fn, deps, inc, sw in tasks:
                        if Prog.SELF_WAIT and ch in ("vector", "scalar", "gpsimd") \
                                and cnt.get(ch, 0) > 0 and seen.get(ch, 0) < cnt[ch]:
                            eng.wait_ge(sems[ch], cnt[ch])
                            seen[ch] = cnt[ch]
                        for (dch, dcnt) in deps:
                            if CH_ENGINE[dch] == eng_name and dch not in LANE_CH \
                                    and ch not in LANE_CH:
                                continue
                            if seen.get(dch, 0) >= dcnt:
                                continue
                            eng.wait_ge(sems[dch], dcnt)
                            seen[dch] = dcnt
                        if fn is not None:
                            fn(eng).then_inc(sems[ch], inc)
                            cnt[ch] = cnt.get(ch, 0) + inc
                return emit

            getattr(block, eng_name)(mk(eng_name, tasks))

    def finish(self):
        self.eng_tasks["sync"].append((None, None, self.frontier(), 0, True))


class Arena:
    def __init__(self, t, total):
        self.t, self.total, self.off, self.marks = t, total, 0, []

    def alloc(self, cols, rows=128):
        a = self.off
        self.off += cols
        assert self.off <= self.total, f"arena overflow {self.off}>{self.total}"
        return self.t[0:rows, a:a + cols]

    def push(self):
        self.marks.append(self.off)

    def pop(self):
        self.off = self.marks.pop()


def bcast(col, n):
    return col.to_broadcast([col.shape[0], n])


def build_nc(tap=False):
    nc = bass.Bass("TRN2", target_bir_lowering=False)
    di = {}

    def dinp(name, shape):
        di[name] = nc.dram_tensor(name, shape, F32, kind="ExternalInput").ap()

    dinp("xt", [128, KT * XC])
    dinp("xp", [128, KT * XC])
    dinp("w_in", [128, NH * KT * 3 * 128])
    dinp("w_rt", [128, KT * 8])
    dinp("w_sg", [128, KT * 1])
    dinp("w_shgu", [128, SHB * FT * 2 * KT * 128])
    dinp("w_shdn", [128, SHB * 16 * FT * 128])
    dinp("w_egu", [128, NE * FT * 2 * KT * 128])
    dinp("w_edn", [128, NE * 16 * FT * 128])
    dinp("w_mb", [128, 16 * KT * 128])
    dinp("w_out", [128, 16 * KT * 128])
    dinp("cw", [128, 16 * 4])
    dinp("cb", [128, 16])
    dinp("nsc", [128, 16])
    dinp("dtb", [128, 16])
    dinp("mflag", [128, 1])
    dinp("iota8", [128, 8])
    dinp("ident", [128, 128])
    dinp("esel", [8, 1024])
    dinp("rope", [128, 8 * 1024])

    y_d = nc.dram_tensor("y", [128, KT * TC], F32, kind="ExternalOutput").ap()
    skind = dict(kind="ExternalOutput") if tap else {}
    scr_co = nc.dram_tensor("scr_co", [128, NH * TC], F32, **skind).ap()
    scr_el = nc.dram_tensor("scr_el", [128, NH * TC], F32, **skind).ap()
    scr_gz = nc.dram_tensor("scr_gz", [128, NH * TC], F32, **skind).ap()
    tp = {}
    if tap:
        for name, shape, dt in [
            ("t_xnbf", [128, KT * TC], BF16), ("t_xc", [128, KT * TC], BF16),
            ("t_rows", [1, 2 * XC], F32), ("t_bdbo", [128, 176], F32),
            ("t_cmb", [128, 64], F32), ("t_acc", [128, KT * TC], F32),
            ("t_fin", [128, KT * TC], BF16), ("t_vmg", [128, KT * TC], BF16),
            ("t_shonly", [128, KT * TC], F32),
        ]:
            tp[name] = nc.dram_tensor(name, shape, dt, kind="ExternalOutput").ap()

    stack = contextlib.ExitStack()
    sb = lambda name, shp, dt=F32: stack.enter_context(nc.sbuf_tensor(name, shp, dt))
    psb = lambda name, shp: stack.enter_context(nc.psum_tensor(name, shp, F32))

    with stack:
        arf = Arena(sb("arf", [128, 23800], F32), 23800)
        arb = Arena(sb("arb", [128, 34816], BF16), 34816)
        wg = [sb("wg0", [128, 6144], BF16), sb("wg1", [128, 6144], BF16)]
        wd = [sb("wd0", [128, 1408], BF16), sb("wd1", [128, 1408], BF16)]
        cw = sb("cw_t", [128, 64]); cb = sb("cb_t", [128, 16])
        nsc = sb("nsc_t", [128, 16]); dtb = sb("dtb_t", [128, 16])
        mflag = sb("mflag_t", [128, 1]); m1f = sb("m1f_t", [128, 1])
        iota = sb("iota_t", [128, 8]); ident = sb("ident_t", [128, 128])
        ones_f = sb("ones_f", [1, 128], F32)
        ones_bf = sb("ones_bf", [128, 1], BF16)
        ones_f1 = sb("ones_f1", [128, 1], F32)
        bd_own = sb("bd_own", [128, 64]); bo_own = sb("bo_own", [128, 64])
        pstat = sb("pstat", [128, 48])
        m_row = sb("m_row", [1, XC]); r_row = sb("r_row", [1, XC])
        kc = sb("kconst", [128, 6], F32)
        wrt_f = sb("wrt_f", [128, 128], F32)
        lg_s = sb("lg_s", [8, 1024], F32)
        xnbf = arb.alloc(KT * TC)
        xcbf = arb.alloc(KT * TC)
        spare_bf = arb.alloc(2048)

        PS = [psb("ps%d" % i, [128, 512]) for i in range(6)]
        PR0 = psb("pr0", [1, 512])
        PR1 = psb("pr1", [1, 512])

        sems = {}
        with contextlib.ExitStack() as semstack:
            for c in CH_ENGINE:
                sems[c] = semstack.enter_context(nc.semaphore("s_" + c))
            with nc.Block() as block:
                p = Prog()
                _emit(p, tap, locals())
                p.finish()
                p.build(block, sems)
    return nc


def _emit(p, tap, g):
    di = g["di"]
    arf = g["arf"]
    PS, PR0, PR1 = g["PS"], g["PR0"], g["PR1"]
    wg, wd = g["wg"], g["wd"]
    xnbf, xcbf, spare_bf = g["xnbf"], g["xcbf"], g["spare_bf"]
    cw, cb, nsc, dtb = g["cw"], g["cb"], g["nsc"], g["dtb"]
    mflag, m1f, iota, ident = g["mflag"], g["m1f"], g["iota"], g["ident"]
    ones_f, ones_bf, ones_f1 = g["ones_f"], g["ones_bf"], g["ones_f1"]
    bd_own, bo_own, pstat = g["bd_own"], g["bo_own"], g["pstat"]
    m_row, r_row = g["m_row"], g["r_row"]
    y_d, scr_co, scr_el, scr_gz, tp = g["y_d"], g["scr_co"], g["scr_el"], g["scr_gz"], g["tp"]

    wrt_f, lg_s = g["wrt_f"], g["lg_s"]
    for t, name in [(cw, "cw"), (cb, "cb"), (nsc, "nsc"), (dtb, "dtb"),
                    (mflag, "mflag"), (iota, "iota8"), (ident, "ident"),
                    (wrt_f, "w_rt")]:
        p.task("sync", lambda e, t=t, n=name: e.dma_start(out=t[:, :], in_=di[n][:, :]))
    consts = p.now("sync")
    kc = g["kc"]
    for i, v in [(0, 1e-5), (1, 1e-6), (2, 1.0), (3, -0.5), (4, 0.0)]:
        p.task("vector", lambda e, i=i, v=v: e.memset(kc[:, i:i + 1], v))
    p.task("vector", lambda e: e.memset(ones_f[:, :], 1.0))
    p.task("vector", lambda e: e.memset(ones_bf[:, :], 1.0))
    p.task("vector", lambda e: e.memset(ones_f1[:, :], 1.0))
    p.task("vector", lambda e: e.tensor_scalar(m1f[:, :], mflag[:, :], -1.0, 1.0,
                                               ALU.mult, ALU.add), after=[consts])

    arf.push()
    rope = {}
    rtmp = arf.alloc(1024)
    rfold = None
    for i, nm in enumerate(["ct", "st", "ctp", "stp"]):
        rope[nm] = arf.alloc(1024)
        t1 = p.task("sync", lambda e, i=i, nm=nm: e.dma_start(
            out=rope[nm][:, :], in_=di["rope"][:, i * 1024:(i + 1) * 1024]))
        t2 = p.task("sync", lambda e, i=i: e.dma_start(
            out=rtmp[:, :], in_=di["rope"][:, (4 + i) * 1024:(5 + i) * 1024]),
            after=[rfold])
        rfold = p.task("vector", lambda e, nm=nm: e.tensor_tensor(
            rope[nm][:, :], rope[nm][:, :], rtmp[:, :], ALU.add), after=[t1, t2])

    def phase_p0(xname, own):
        p.guard = False
        arf.push()
        xchunk = [arf.alloc(KT * 259), arf.alloc(KT * 259)]
        bcm = [arf.alloc(259), arf.alloc(259)]
        bcr = [arf.alloc(259), arf.alloc(259)]
        bcrr = [arf.alloc(259), arf.alloc(259)]
        t_sq = [arf.alloc(259), arf.alloc(259)]
        t_px = [arf.alloc(259), arf.alloc(259)]
        t_a = [arf.alloc(259), arf.alloc(259)]
        t_xn2 = [arf.alloc(259), arf.alloc(259)]
        t_cv = [arf.alloc(256), arf.alloc(256)]
        rows_m = arf.alloc(259, rows=1); rows_v = arf.alloc(259, rows=1)
        rows_r = arf.alloc(259, rows=1); rows_u = arf.alloc(259, rows=1)
        rows_t = arf.alloc(259, rows=1)
        x3 = di[xname].rearrange("p (kt t) -> p kt t", kt=KT)

        prev_apply = [None, None]
        sq_use = [None, None]
        row_evac = None
        bc_use = [None, None]
        rt_guard = [None]
        prev_g1 = [(), ()]
        for c in range(NCH):
            par = c % 2
            c0 = c * 256
            loads = []
            last_mm = None
            for dt in range(KT):
                bi = dt % 2
                lx = p.task("sync" if dt % 2 == 0 else "sdma",
                            lambda e, par=par, dt=dt, c0=c0: e.dma_start(
                    out=xchunk[par][:, dt * 259:(dt + 1) * 259],
                    in_=x3[:, dt, c0:c0 + 259]), after=[prev_apply[par]])
                loads.append(lx)
                sq = p.task("scalar", lambda e, par=par, dt=dt, bi=bi: e.activation(
                    t_sq[bi][:, :], xchunk[par][:, dt * 259:(dt + 1) * 259], AF.Square),
                    after=[lx, sq_use[bi]])
                p.task("tensor", lambda e, par=par, dt=dt: e.matmul(
                    PR0[0:1, 0:259], ones_f1[:, :], xchunk[par][:, dt * 259:(dt + 1) * 259],
                    start=(dt == 0), stop=(dt == KT - 1)),
                    after=[lx, row_evac, consts])
                last_mm = p.task("tensor", lambda e, bi=bi, dt=dt: e.matmul(
                    PR1[0:1, 0:259], ones_f1[:, :], t_sq[bi][:, :],
                    start=(dt == 0), stop=(dt == KT - 1)), after=[sq])
                sq_use[bi] = last_mm
            p.task("vector", lambda e: e.tensor_scalar(
                rows_m[:, :], PR0[0:1, 0:259], 1.0 / 2048, None, ALU.mult),
                after=[last_mm])
            row_evac = p.task("vector", lambda e: e.tensor_scalar(
                rows_v[:, :], PR1[0:1, 0:259], 1.0 / 2048, None, ALU.mult))
            p.task("vector", lambda e: e.tensor_tensor(
                rows_t[:, :], rows_m[:, :], rows_m[:, :], ALU.mult))
            p.task("vector", lambda e: e.tensor_tensor(
                rows_v[:, :], rows_v[:, :], rows_t[:, :], ALU.subtract))
            vtok = p.now("vector")
            p.task("scalar", lambda e: e.activation(
                rows_r[:, :], rows_v[:, :], AF.Ln, bias=kc[0:1, 0:1]), after=[vtok])
            s2 = p.task("scalar", lambda e: e.activation(
                rows_r[:, :], rows_r[:, :], AF.Exp, scale=kc[0:1, 3:4]))
            p.task("vector", lambda e: e.tensor_tensor(
                rows_u[:, :], rows_r[:, :], rows_r[:, :], ALU.mult), after=[s2])
            u2 = p.task("vector", lambda e: e.tensor_tensor(
                rows_u[:, :], rows_u[:, :], rows_v[:, :], ALU.mult))
            p.task("scalar", lambda e: e.activation(
                rows_u[:, :], rows_u[:, :], AF.Ln, bias=kc[0:1, 1:2]), after=[u2])
            s4 = p.task("scalar", lambda e: e.activation(
                rows_u[:, :], rows_u[:, :], AF.Exp, scale=kc[0:1, 3:4]))
            p.task("vector", lambda e: e.tensor_tensor(
                rows_u[:, :], rows_u[:, :], rows_r[:, :], ALU.mult), after=[s4])
            if own:
                p.task("vector", lambda e, c0=c0: e.tensor_copy(
                    m_row[:, c0:c0 + 259], rows_m[:, :]))
                p.task("vector", lambda e, c0=c0: e.tensor_copy(
                    r_row[:, c0:c0 + 259], rows_r[:, :]))
            rows_done = p.now("vector")
            bt = []
            for psi, row, dst in [(3, rows_m, bcm[par]), (4, rows_r, bcr[par]),
                                  (5, rows_u, bcrr[par])]:
                mm = p.task("tensor", lambda e, psi=psi, row=row: e.matmul(
                    PS[psi][:, 0:259], ones_f[:, :], row[:, :], start=True, stop=True),
                    after=[rows_done, bc_use[par]])
                bt.append(p.task("vector", lambda e, psi=psi, dst=dst: e.tensor_copy(
                    dst[:, :], PS[psi][:, 0:259]), after=[mm]))
            bc_done = bt[-1]
            bc_use[par] = bc_done
            prev_sil = [None, None]
            bcm_c, bcr_c, bcrr_c = bcm[par], bcr[par], bcrr[par]
            for dt in range(KT):
                bi = dt % 2
                xsl = xchunk[par][:, dt * 259:(dt + 1) * 259]
                g1 = p.task("gpsimd", lambda e, xsl=xsl, bi=bi, bcm_c=bcm_c: e.tensor_tensor(
                    t_px[bi][:, :], xsl, bcm_c[:, :], ALU.subtract),
                    after=[bc_done, loads[dt]] + list(prev_g1[bi]))
                p.task("gpsimd", lambda e, bi=bi, bcrr_c=bcrr_c: e.tensor_tensor(
                    t_a[bi][:, :], t_px[bi][:, :], bcrr_c[:, :], ALU.mult))
                p.task("gpsimd", lambda e, dt=dt, bi=bi: e.tensor_tensor(
                    t_a[bi][:, :], t_a[bi][:, :], bcast(nsc[:, dt:dt + 1], 259), ALU.mult))
                gt = p.now("gpsimd")
                cl = p.task("vector", lambda e, bi=bi: e.tensor_scalar(
                    t_xn2[bi][:, :], t_a[bi][:, :], 60000.0, -60000.0, ALU.min, ALU.max),
                    after=[gt])
                cv0 = p.task("gpsimd", lambda e, dt=dt, bi=bi: e.tensor_tensor(
                    t_cv[bi][:, :], t_xn2[bi][:, 0:256], bcast(cw[:, dt * 4:dt * 4 + 1], 256),
                    ALU.mult), after=[cl, prev_sil[bi]])
                cvb = None
                for k in (1, 2, 3):
                    cvb = p.task("vector", lambda e, dt=dt, k=k, bi=bi: e.scalar_tensor_tensor(
                        t_cv[bi][:, :], t_xn2[bi][:, k:k + 256], cw[:, dt * 4 + k:dt * 4 + k + 1],
                        t_cv[bi][:, :], ALU.mult, ALU.add), after=[cv0])
                prev_sil[bi] = p.task("scalar", lambda e, dt=dt, c0=c0, bi=bi: e.activation(
                    xcbf[:, dt * TC + c0:dt * TC + c0 + 256], t_cv[bi][:, :], AF.Silu,
                    bias=cb[:, dt:dt + 1]), after=[cvb])
                if own:
                    xnc = p.task("vector", lambda e, dt=dt, c0=c0, bi=bi, bcr_c=bcr_c: e.tensor_tensor(
                        xnbf[:, dt * TC + c0:dt * TC + c0 + 256], t_px[bi][:, 3:259],
                        bcr_c[:, 3:259], ALU.mult))
                    xf32 = p.task("vector", lambda e, bi=bi, bcr_c=bcr_c: e.tensor_tensor(
                        t_a[bi][:, 0:256], t_px[bi][:, 3:259], bcr_c[:, 3:259], ALU.mult),
                        after=[cl])
                    rt_mm = p.task("tensor", lambda e, dt=dt, bi=bi: e.matmul(
                        PS[0][0:8, 0:256], wrt_f[:, dt * 8:(dt + 1) * 8],
                        t_a[bi][:, 0:256], start=(dt == 0), stop=(dt == KT - 1)),
                        after=[xf32, rt_guard[0]])
                    prev_g1[bi] = (xnc, rt_mm)
                else:
                    prev_g1[bi] = (cl,)
            if own:
                rt_guard[0] = p.task("vector", lambda e, c0=c0: e.tensor_copy(
                    lg_s[0:8, c0:c0 + 256], PS[0][0:8, 0:256]), after=[prev_g1[1][1]])
            prev_apply[par] = p.now("scalar")
        arf.pop()
        p.guard = True

    def phase_front(own):
        p.guard = False
        arf.push()
        TL = []
        for _ in range(2):
            t = dict(dtt=arf.alloc(TC), tmpa=arf.alloc(TC), tmpb=arf.alloc(TC),
                     vv=arf.alloc(TC), vsw=arf.alloc(TC), vr=arf.alloc(TC),
                     Lt=arf.alloc(TC), eLs=arf.alloc(TC), co=arf.alloc(TC),
                     sm=arf.alloc(40))
            TL.append(t)
        ct = rope["ct" if own else "ctp"]
        stt = rope["st" if own else "stp"]
        nblk = 3 if own else 2
        wtok = [None, None]
        use_tok = [None, None]
        ps_free = [None] * 6
        pv = [dict(vin=None, el=None, el_dma=None, co_dma=None, gz_dma=None,
                   swp=None, swp2=None, r1=None, pr=None, cot=None, gz1=None) for _ in range(2)]
        for h in range(NH):
            par = h % 2
            T = TL[par]
            dtt, tmpa, tmpb = T["dtt"], T["tmpa"], T["tmpb"]
            vv, vsw, vr = T["vv"], T["vsw"], T["vr"]
            Lt, eLs, co, sm = T["Lt"], T["eLs"], T["co"], T["sm"]
            l0 = sm[:, 0:4]; nl0 = sm[:, 4:8]; pbd = sm[:, 8:12]; pbo = sm[:, 12:16]
            pc4 = sm[:, 16:20]; e4 = sm[:, 20:24]; en4 = sm[:, 24:28]
            sd4 = sm[:, 28:32]
            V = pv[par]
            wtok[par] = p.task("gdma", lambda e, h=h, par=par: e.dma_start(
                out=wg[par][:, :], in_=di["w_in"][:, h * 6144:(h + 1) * 6144]),
                after=[use_tok[par]])
            mmt = [None] * 6
            for blk in range(nblk):
                for tc2 in range(2):
                    psi = blk * 2 + tc2

                    def mmfn(e, par=par, blk=blk, tc2=tc2, psi=psi):
                        for kt in range(KT):
                            r = e.matmul(
                                PS[psi][:, :],
                                wg[par][:, kt * 384 + blk * 128:kt * 384 + (blk + 1) * 128],
                                xcbf[:, kt * TC + tc2 * 512:kt * TC + tc2 * 512 + 512],
                                start=(kt == 0), stop=(kt == KT - 1))
                        return r
                    mmt[psi] = p.task("tensor", mmfn,
                                      after=[wtok[par], ps_free[psi]])
            use_tok[par] = mmt[2 * nblk - 1]
            ev = []
            for tc2 in range(2):
                ev.append(p.task("scalar", lambda e, tc2=tc2, h=h, dtt=dtt: e.activation(
                    dtt[:, tc2 * 512:(tc2 + 1) * 512], PS[tc2][:, :],
                    AF.Identity, bias=dtb[:, h:h + 1]),
                    after=[mmt[tc2], V["vin"]]))
            ps_free[0] = ev[0]; ps_free[1] = ev[1]
            vev = []
            for tc2 in range(2):
                vev.append(p.task("scalar", lambda e, tc2=tc2, vv=vv: e.activation(
                    vv[:, tc2 * 512:(tc2 + 1) * 512], PS[2 + tc2][:, :],
                    AF.Identity, bias=kc[:, 4:5]),
                    after=[mmt[2 + tc2], V["gz_dma"], V["swp"], V.get("swp2"), V["r1"]]))
            ps_free[2] = vev[0]; ps_free[3] = vev[1]
            swp1 = p.task("sync", lambda e, vsw=vsw, vv=vv: e.dma_start(
                out=vsw[0:64, :], in_=vv[64:128, :]), after=[vev[1], V["cot"]])
            swp2 = p.task("sdma", lambda e, vsw=vsw, vv=vv: e.dma_start(
                out=vsw[64:128, :], in_=vv[0:64, :]), after=[vev[1], V["cot"]])
            r1 = p.task("gpsimd", lambda e, vr=vr, vv=vv: e.tensor_tensor(
                vr[:, :], vv[:, :], ct[:, :], ALU.mult), after=[vev[1], rfold, V["pr"]])
            if own:
                gz = []
                for tc2 in range(2):
                    gz.append(p.task("scalar", lambda e, tc2=tc2, vv=vv: e.activation(
                        vv[:, tc2 * 512:(tc2 + 1) * 512], PS[4 + tc2][:, :],
                        AF.Identity, bias=kc[:, 4:5]),
                        after=[mmt[4 + tc2], swp1, swp2, r1]))
                ps_free[4] = gz[0]; ps_free[5] = gz[1]
                V["gz_dma"] = p.task("sync", lambda e, h=h, vv=vv: e.dma_start(
                    out=scr_gz[:, h * TC:(h + 1) * TC], in_=vv[:, :]), after=[gz[1]])
            p.task("scalar", lambda e, tmpa=tmpa, dtt=dtt: e.activation(
                tmpa[:, :], dtt[:, :], AF.Abs),
                after=[V["el_dma"], V["pr"]])
            p.task("scalar", lambda e, tmpa=tmpa: e.activation(
                tmpa[:, :], tmpa[:, :], AF.Exp, scale=-1.0))
            lnt = p.task("scalar", lambda e, tmpa=tmpa: e.activation(
                tmpa[:, :], tmpa[:, :], AF.Ln, bias=kc[:, 2:3]))
            p.task("vector", lambda e, tmpb=tmpb, dtt=dtt: e.tensor_scalar(
                tmpb[:, :], dtt[:, :], 0.0, None, ALU.max), after=[ev[1]])
            p.task("vector", lambda e, dtt=dtt, tmpa=tmpa, tmpb=tmpb: e.tensor_tensor(
                dtt[:, :], tmpa[:, :], tmpb[:, :], ALU.add), after=[lnt])
            clt = p.task("vector", lambda e, dtt=dtt: e.tensor_scalar(
                dtt[:, :], dtt[:, :], 2.0, 0.001, ALU.min, ALU.max))
            la = p.task("vector", lambda e, tmpb=tmpb, dtt=dtt: e.tensor_scalar(
                tmpb[:, :], dtt[:, :], -1.0, None, ALU.mult))
            p.task("gpsimd", lambda e, vsw=vsw: e.tensor_tensor(
                vsw[:, :], vsw[:, :], stt[:, :], ALU.mult), after=[swp1, swp2])
            p.task("gpsimd", lambda e, vr=vr, vsw=vsw: e.tensor_tensor(
                vr[:, :], vr[:, :], vsw[:, :], ALU.add))
            vin = p.task("gpsimd", lambda e, vr=vr, dtt=dtt: e.tensor_tensor(
                vr[:, :], vr[:, :], dtt[:, :], ALU.mult), after=[clt])
            V["vin"] = vin
            V["swp"] = swp1
            V["swp2"] = swp2
            V["r1"] = r1
            for c in range(NCH):
                p.task("vector", lambda e, c=c, Lt=Lt, tmpb=tmpb: e.tensor_tensor_scan(
                    Lt[:, c * CH:(c + 1) * CH], tmpb[:, c * CH:(c + 1) * CH],
                    tmpb[:, c * CH:(c + 1) * CH], 0.0, ALU.add, ALU.bypass),
                    after=[la, V["el"]])
            p.task("vector", lambda e, Lt=Lt: e.tensor_scalar(
                Lt[:, :], Lt[:, :], 0.0, -20.0, ALU.min, ALU.max))
            bdt = bd_own[:, h * 4:(h + 1) * 4] if own else pbd
            bot = bo_own[:, h * 4:(h + 1) * 4] if own else pbo
            p.task("vector", lambda e, bdt=bdt, Lt=Lt: e.tensor_copy(bdt, Lt[:, 255:TC:256]), sw=True)
            p.task("vector", lambda e, l0=l0, Lt=Lt: e.tensor_copy(l0[:, :], Lt[:, 0:TC:256]), sw=True)
            nl0t = p.task("vector", lambda e, nl0=nl0, l0=l0: e.tensor_scalar(
                nl0[:, :], l0[:, :], -1.0, None, ALU.mult), sw=True)
            for c in range(NCH):
                p.task("scalar", lambda e, c=c, tmpa=tmpa, Lt=Lt, l0=l0: e.activation(
                    tmpa[:, c * CH:(c + 1) * CH], Lt[:, c * CH:(c + 1) * CH],
                    AF.Exp, bias=l0[:, c:c + 1], scale=-1.0), after=[nl0t])
                p.task("scalar", lambda e, c=c, eLs=eLs, Lt=Lt, nl0=nl0: e.activation(
                    eLs[:, c * CH:(c + 1) * CH], Lt[:, c * CH:(c + 1) * CH],
                    AF.Exp, bias=nl0[:, c:c + 1], scale=1.0))
            sct = p.now("scalar")
            pr = p.task("vector", lambda e, tmpa=tmpa, vr=vr: e.tensor_tensor(
                tmpa[:, :], tmpa[:, :], vr[:, :], ALU.mult), after=[sct, vin])
            V["pr"] = pr
            for c in range(NCH):
                p.task("vector", lambda e, c=c, vsw=vsw, tmpa=tmpa: e.tensor_tensor_scan(
                    vsw[:, c * CH:(c + 1) * CH], tmpa[:, c * CH:(c + 1) * CH],
                    tmpa[:, c * CH:(c + 1) * CH], 0.0, ALU.add, ALU.bypass))
            cum_tok = p.now("vector")
            cot = p.task("vector", lambda e, co=co, eLs=eLs, vsw=vsw: e.tensor_tensor(
                co[:, :], eLs[:, :], vsw[:, :], ALU.mult), after=[V["co_dma"]])
            V["cot"] = cot
            p.task("vector", lambda e, bot=bot, co=co: e.tensor_copy(bot, co[:, 255:TC:256]), sw=True)
            if own:
                el = p.task("scalar", lambda e, tmpa=tmpa, Lt=Lt: e.activation(
                    tmpa[:, :], Lt[:, :], AF.Exp), after=[cum_tok])
                V["el"] = el
                V["co_dma"] = p.task("sync", lambda e, h=h, co=co: e.dma_start(
                    out=scr_co[:, h * TC:(h + 1) * TC], in_=co[:, :]), after=[cot])
                V["el_dma"] = p.task("sync", lambda e, h=h, tmpa=tmpa: e.dma_start(
                    out=scr_el[:, h * TC:(h + 1) * TC], in_=tmpa[:, :]), after=[el])
            else:
                p.guard = True
                p.task("vector", lambda e, pc4=pc4, pbd=pbd: e.tensor_tensor_scan(
                    pc4[:, :], pbd, pbd, 0.0, ALU.add, ALU.bypass))
                p.task("vector", lambda e, e4=e4, pc4=pc4: e.tensor_scalar(
                    e4[:, :], pc4[:, :], 0.0, -80.0, ALU.min, ALU.max),
                    after=[V["el"]])
                p.task("vector", lambda e, e4=e4, pbd=pbd: e.tensor_tensor(
                    e4[:, :], e4[:, :], bcast(pbd[:, 0:1], 4), ALU.subtract))
                ncd = p.task("vector", lambda e, e4=e4: e.tensor_scalar(
                    e4[:, :], e4[:, :], 0.0, -20.0, ALU.min, ALU.max))
                ex = p.task("scalar", lambda e, en4=en4, e4=e4: e.activation(
                    en4[:, :], e4[:, :], AF.Exp, scale=-1.0), after=[ncd])
                V["el"] = ex
                p.task("vector", lambda e, sd4=sd4, pbo=pbo, en4=en4: e.tensor_tensor(
                    sd4[:, :], pbo, en4[:, :], ALU.mult), after=[ex])
                p.task("vector", lambda e, h=h, pbd=pbd: e.tensor_copy(
                    pstat[:, h * 3:h * 3 + 1], pbd[:, 0:1]))
                p.task("vector", lambda e, h=h, pc4=pc4: e.tensor_copy(
                    pstat[:, h * 3 + 1:h * 3 + 2], pc4[:, 3:4]))
                p.task("vector", lambda e, h=h, sd4=sd4: e.tensor_reduce(
                    pstat[:, h * 3 + 2:h * 3 + 3], sd4[:, :], AX, ALU.add))
        arf.pop()

    def phase_ffn(acc_cols):
        p.guard = False
        arf.push()
        esel = arf.alloc(TC, rows=8)
        lesel = p.task("sync", lambda e: e.dma_start(out=esel[:, :], in_=di["esel"][:, :]))
        cmb_fm = arf.alloc(TC, rows=8)
        zg = arf.alloc(TC, rows=1)
        cmb_tm = arf.alloc(64)
        sg_bc = arf.alloc(TC)
        cmb_bc = arf.alloc(TC)
        sil = arf.alloc(512)
        s8a = arf.alloc(8); s8b = arf.alloc(8); s8c = arf.alloc(8)
        sc1 = arf.alloc(4)
        hblk = xcbf[:, 0:FT * TC]
        psg = {i: None for i in range(6)}

        lg = lg_s
        lg_done = None
        sg_w = spare_bf[:, 1164:1180]
        lsg = p.task("gdma", lambda e: e.dma_start(out=sg_w[:, :], in_=di["w_sg"][:, :]))
        for tc2 in range(2):
            def mms(e, tc2=tc2):
                for kt in range(KT):
                    r = e.matmul(PS[1][0:1, :], sg_w[:, kt:kt + 1],
                                 xnbf[:, kt * TC + tc2 * 512:kt * TC + tc2 * 512 + 512],
                                 start=(kt == 0), stop=(kt == KT - 1))
                return r
            mm = p.task("tensor", mms, after=[lsg, psg[1]])
            psg[1] = p.task("vector", lambda e, tc2=tc2: e.tensor_copy(
                zg[:, tc2 * 512:(tc2 + 1) * 512], PS[1][0:1, :]), after=[mm])
        sgm = p.task("scalar", lambda e: e.activation(zg[:, :], zg[:, :], AF.Sigmoid),
                     after=[psg[1]])
        for tc2 in range(2):
            mm = p.task("tensor", lambda e, tc2=tc2: e.matmul(
                PS[2][:, :], ones_f[:, :], zg[0:1, tc2 * 512:(tc2 + 1) * 512],
                start=True, stop=True), after=[sgm, psg[2]])
            psg[2] = p.task("vector", lambda e, tc2=tc2: e.tensor_copy(
                sg_bc[:, tc2 * 512:(tc2 + 1) * 512], PS[2][:, :]), after=[mm])
        sg_done = psg[2]
        p.guard = True
        for gi in range(8):
            tr = p.task("tensor", lambda e, gi=gi: e.transpose(
                PS[3][:, 0:8], lg[:, gi * 128:(gi + 1) * 128], ident[0:8, 0:8]),
                after=[lg_done, psg[3]])
            psg[3] = p.task("vector", lambda e: e.tensor_copy(s8a[:, :], PS[3][:, 0:8]),
                            after=[tr])
            p.task("vector", lambda e: e.tensor_reduce(sc1[:, 0:1], s8a[:, :], AX, ALU.max))
            p.task("vector", lambda e: e.tensor_scalar(
                sc1[:, 1:2], sc1[:, 0:1], -1.0, None, ALU.mult))
            vt = p.now("vector")
            et = p.task("scalar", lambda e: e.activation(
                s8b[:, :], s8a[:, :], AF.Exp, bias=sc1[:, 1:2]), after=[vt])
            p.task("vector", lambda e: e.tensor_reduce(
                sc1[:, 0:1], s8b[:, :], AX, ALU.add), after=[et])
            p.task("vector", lambda e: e.reciprocal(sc1[:, 1:2], sc1[:, 0:1]))
            p.task("vector", lambda e: e.tensor_tensor(
                s8a[:, :], s8b[:, :], bcast(sc1[:, 1:2], 8), ALU.mult))
            p.task("vector", lambda e: e.tensor_reduce(
                sc1[:, 0:1], s8a[:, :], AX, ALU.max))
            p.task("vector", lambda e: e.tensor_tensor(
                s8b[:, :], s8a[:, :], bcast(sc1[:, 0:1], 8), ALU.subtract))
            p.task("vector", lambda e: e.scalar_tensor_tensor(
                s8b[:, :], s8b[:, :], -BIG, iota[:, :], ALU.mult, ALU.add))
            p.task("vector", lambda e: e.tensor_reduce(
                sc1[:, 1:2], s8b[:, :], AX, ALU.min))
            p.task("vector", lambda e: e.tensor_tensor(
                s8b[:, :], iota[:, :], bcast(sc1[:, 1:2], 8), ALU.subtract))
            p.task("vector", lambda e: e.tensor_tensor(
                s8b[:, :], s8b[:, :], s8b[:, :], ALU.mult))
            p.task("vector", lambda e: e.tensor_scalar(
                s8b[:, :], s8b[:, :], -1.0, 1.0, ALU.mult, ALU.add))
            p.task("vector", lambda e: e.tensor_scalar(
                s8b[:, :], s8b[:, :], 0.0, None, ALU.max))
            p.task("vector", lambda e: e.scalar_tensor_tensor(
                s8c[:, :], s8b[:, :], -BIG, s8a[:, :], ALU.mult, ALU.add))
            p.task("vector", lambda e: e.tensor_reduce(
                sc1[:, 2:3], s8c[:, :], AX, ALU.max))
            p.task("vector", lambda e: e.tensor_tensor(
                s8c[:, :], s8c[:, :], bcast(sc1[:, 2:3], 8), ALU.subtract))
            p.task("vector", lambda e: e.scalar_tensor_tensor(
                s8c[:, :], s8c[:, :], -BIG, iota[:, :], ALU.mult, ALU.add))
            p.task("vector", lambda e: e.tensor_reduce(
                sc1[:, 3:4], s8c[:, :], AX, ALU.min))
            p.task("vector", lambda e: e.tensor_tensor(
                s8c[:, :], iota[:, :], bcast(sc1[:, 3:4], 8), ALU.subtract))
            p.task("vector", lambda e: e.tensor_tensor(
                s8c[:, :], s8c[:, :], s8c[:, :], ALU.mult))
            p.task("vector", lambda e: e.tensor_scalar(
                s8c[:, :], s8c[:, :], -1.0, 1.0, ALU.mult, ALU.add))
            p.task("vector", lambda e: e.tensor_scalar(
                s8c[:, :], s8c[:, :], 0.0, None, ALU.max))
            p.task("vector", lambda e: e.tensor_tensor(
                s8b[:, :], s8b[:, :], bcast(sc1[:, 0:1], 8), ALU.mult))
            p.task("vector", lambda e, gi=gi: e.scalar_tensor_tensor(
                cmb_tm[:, gi * 8:(gi + 1) * 8], s8c[:, :], sc1[:, 2:3], s8b[:, :],
                ALU.mult, ALU.add))
        cmb_done = p.now("vector")
        if tap:
            p.task("sync", lambda e: e.dma_start(out=tp["t_cmb"][:, :], in_=cmb_tm[:, :]),
                   after=[cmb_done])
        for gi in range(8):
            tr = p.task("tensor", lambda e, gi=gi: e.transpose(
                PS[3][0:8, 0:128], cmb_tm[:, gi * 8:(gi + 1) * 8], ident[:, :]),
                after=[cmb_done, psg[3]])
            psg[3] = p.task("vector", lambda e, gi=gi: e.tensor_copy(
                cmb_fm[:, gi * 128:(gi + 1) * 128], PS[3][0:8, 0:128]), after=[tr])
        cmb_fm_done = psg[3]
        p.guard = False

        wtok = [None, None]
        wuse = [None, None]
        dtok = [None, None]
        duse = [None, None]
        huse = [None]
        acc_tok = [None] * 32

        sil_use = [None]

        def gu_block(wname, base_col, scale_cmb, cmb_tok):
            hw = []
            for ft in range(FT):
                par = ft % 2
                wtok[par] = p.task("gdma", lambda e, par=par, wname=wname, base_col=base_col, ft=ft: e.dma_start(
                    out=wg[par][:, 0:4096],
                    in_=di[wname][:, base_col + ft * 4096:base_col + (ft + 1) * 4096]),
                    after=[wuse[par]])
                for tc2 in range(2):
                    def mmg(e, par=par, tc2=tc2):
                        for kt in range(KT):
                            r = e.matmul(PS[0 + tc2][:, :],
                                         wg[par][:, kt * 128:(kt + 1) * 128],
                                         xnbf[:, kt * TC + tc2 * 512:kt * TC + tc2 * 512 + 512],
                                         start=(kt == 0), stop=(kt == KT - 1))
                        return r

                    def mmu(e, par=par, tc2=tc2):
                        for kt in range(KT):
                            r = e.matmul(PS[2 + tc2][:, :],
                                         wg[par][:, 2048 + kt * 128:2048 + (kt + 1) * 128],
                                         xnbf[:, kt * TC + tc2 * 512:kt * TC + tc2 * 512 + 512],
                                         start=(kt == 0), stop=(kt == KT - 1))
                        return r
                    mg = p.task("tensor", mmg, after=[wtok[par], psg[0 + tc2]])
                    mu = p.task("tensor", mmu, after=[wtok[par], psg[2 + tc2]])
                    sl = p.task("scalar", lambda e, tc2=tc2: e.activation(
                        sil[:, :], PS[0 + tc2][:, :], AF.Silu),
                        after=[mg, sil_use[0]])
                    psg[0 + tc2] = sl
                    if scale_cmb is None:
                        hv = p.task("vector", lambda e, ft=ft, tc2=tc2: e.tensor_tensor(
                            hblk[:, ft * TC + tc2 * 512:ft * TC + tc2 * 512 + 512],
                            sil[:, :], PS[2 + tc2][:, :], ALU.mult),
                            after=[sl, mu, huse[0]])
                    else:
                        hv0 = p.task("vector", lambda e, tc2=tc2: e.tensor_tensor(
                            sil[:, :], sil[:, :], PS[2 + tc2][:, :], ALU.mult),
                            after=[sl, mu])
                        hv = p.task("vector", lambda e, ft=ft, tc2=tc2: e.tensor_tensor(
                            hblk[:, ft * TC + tc2 * 512:ft * TC + tc2 * 512 + 512],
                            sil[:, :], scale_cmb[:, tc2 * 512:(tc2 + 1) * 512], ALU.mult),
                            after=[hv0, huse[0], cmb_tok])
                    sil_use[0] = hv
                    psg[2 + tc2] = hv
                    wuse[par] = hv
                    hw.append(hv)
            return hw

        def dn_block(wname, base_col, first, hw):
            last = []
            for fo in range(16):
                par = fo % 2
                dtok[par] = p.task("gdma", lambda e, par=par, wname=wname, base_col=base_col, fo=fo: e.dma_start(
                    out=wd[par][:, :],
                    in_=di[wname][:, base_col + fo * 1408:base_col + (fo + 1) * 1408]),
                    after=[duse[par]])
                for tc2 in range(2):
                    def mmd(e, par=par, tc2=tc2):
                        for kt in range(FT):
                            r = e.matmul(PS[4 + tc2][:, :],
                                         wd[par][:, kt * 128:(kt + 1) * 128],
                                         hblk[:, kt * TC + tc2 * 512:kt * TC + tc2 * 512 + 512],
                                         start=(kt == 0), stop=(kt == FT - 1))
                        return r
                    md = p.task("tensor", mmd,
                                after=[dtok[par], hw[-1], acc_tok[fo * 2 + tc2], psg[4 + tc2]])
                    asl = acc_cols[:, fo * TC + tc2 * 512:fo * TC + tc2 * 512 + 512]
                    if first:
                        at = p.task("vector", lambda e, asl=asl, tc2=tc2: e.tensor_copy(
                            asl, PS[4 + tc2][:, :]), after=[md])
                    else:
                        at = p.task("vector", lambda e, asl=asl, tc2=tc2: e.tensor_tensor(
                            asl, PS[4 + tc2][:, :], asl, ALU.add), after=[md])
                    psg[4 + tc2] = at
                    acc_tok[fo * 2 + tc2] = at
                    duse[par] = at
                    last.append(at)
            return last

        for b in range(SHB):
            hw = gu_block("w_shgu", b * FT * 4096, None, None)
            dl = dn_block("w_shdn", b * 16 * 1408, b == 0, hw)
            huse[0] = dl[-1]
        shtap = None
        if tap:
            shtap = p.task("sync", lambda e: e.dma_start(
                out=tp["t_shonly"][:, :], in_=acc_cols[:, :]), after=[acc_tok[-1]])
        for fo in range(16):
            for tc2 in range(2):
                asl = acc_cols[:, fo * TC + tc2 * 512:fo * TC + tc2 * 512 + 512]
                t = p.task("vector", lambda e, asl=asl, tc2=tc2: e.tensor_tensor(
                    asl, asl, sg_bc[:, tc2 * 512:(tc2 + 1) * 512], ALU.mult),
                    after=[sg_done, acc_tok[fo * 2 + tc2], shtap])
                acc_tok[fo * 2 + tc2] = t
        for ei in range(NE):
            cbt = None
            for tc2 in range(2):
                mm = p.task("tensor", lambda e, ei=ei, tc2=tc2: e.matmul(
                    PS[4][:, :], esel[:, ei * 128:(ei + 1) * 128],
                    cmb_fm[0:8, tc2 * 512:(tc2 + 1) * 512],
                    start=True, stop=True),
                    after=[cmb_fm_done, lesel, psg[4], duse[0], duse[1]])
                psg[4] = p.task("vector", lambda e, tc2=tc2: e.tensor_copy(
                    cmb_bc[:, tc2 * 512:(tc2 + 1) * 512], PS[4][:, :]), after=[mm])
            cbt = psg[4]
            hw = gu_block("w_egu", ei * FT * 4096, cmb_bc, cbt)
            dl = dn_block("w_edn", ei * 16 * 1408, False, hw)
            huse[0] = dl[-1]
        gt = []
        for fo in range(16):
            gt.append(p.task("scalar", lambda e, fo=fo: e.activation(
                acc_cols[:, fo * TC:(fo + 1) * TC], acc_cols[:, fo * TC:(fo + 1) * TC],
                AF.Silu), after=[acc_tok[fo * 2], acc_tok[fo * 2 + 1]]))
        if tap:
            p.task("sync", lambda e: e.dma_start(out=tp["t_acc"][:, :], in_=acc_cols[:, :]),
                   after=[gt[-1]])
        arf.pop()
        p.guard = True

    def phase_tail(acc_cols):
        p.guard = False
        arf.push()
        mb_t = arf.alloc(TC); rb_t = arf.alloc(TC)
        sm = arf.alloc(48)
        stab = sm[:, 0:1]; pb = sm[:, 1:2]; sb0 = sm[:, 2:3]
        pc4 = sm[:, 4:8]; cd4 = sm[:, 8:12]; E4 = sm[:, 12:16]
        En4 = sm[:, 16:20]; sd4 = sm[:, 20:24]; si4 = sm[:, 24:28]
        ca4 = sm[:, 28:32]
        fin = xcbf
        vmg = xnbf
        x3 = di["xt"].rearrange("p (kt t) -> p kt t", kt=KT)

        bmt = []
        for tc2 in range(2):
            for psi, row, dst in [(0, m_row, mb_t), (1, r_row, rb_t)]:
                mm = p.task("tensor", lambda e, psi=psi, row=row, tc2=tc2: e.matmul(
                    PS[psi][:, :], ones_f[:, :], row[0:1, 3 + tc2 * 512:3 + tc2 * 512 + 512],
                    start=True, stop=True), after=bmt[-1:])
                bmt.append(p.task("vector", lambda e, psi=psi, dst=dst, tc2=tc2: e.tensor_copy(
                    dst[:, tc2 * 512:(tc2 + 1) * 512], PS[psi][:, :]), after=[mm]))
        bdone = bmt[-1]

        arf.push()
        co_t = [arf.alloc(TC), arf.alloc(TC)]
        el_t = [arf.alloc(TC), arf.alloc(TC)]
        stab16 = arf.alloc(16); pb16 = arf.alloc(16); sb16 = arf.alloc(16)
        tmp16 = arf.alloc(16)
        pc64 = arf.alloc(64); cd64 = arf.alloc(64); E64 = arf.alloc(64)
        En64 = arf.alloc(64); sd64 = arf.alloc(64); si64 = arf.alloc(64)
        sh64 = arf.alloc(64); ca64 = arf.alloc(64)

        def v3(ap, b):
            return ap.rearrange("p (a b) -> p a b", b=b)

        def bc3(col16, b):
            return col16.rearrange("p (a b) -> p a b", b=1).to_broadcast([128, 16, b])

        p.guard = True
        p.task("vector", lambda e: e.tensor_tensor(
            stab16[:, :], pstat[:, 0:48:3], bcast(mflag[:, 0:1], 16), ALU.mult))
        p.task("vector", lambda e: e.tensor_tensor(
            tmp16[:, :], bd_own[:, 0:64:4], bcast(m1f[:, 0:1], 16), ALU.mult))
        p.task("vector", lambda e: e.tensor_tensor(
            stab16[:, :], stab16[:, :], tmp16[:, :], ALU.add))
        p.task("vector", lambda e: e.tensor_tensor(
            pb16[:, :], pstat[:, 1:48:3], bcast(mflag[:, 0:1], 16), ALU.mult))
        p.task("vector", lambda e: e.tensor_tensor(
            sb16[:, :], pstat[:, 2:48:3], bcast(mflag[:, 0:1], 16), ALU.mult))
        for h in range(NH):
            p.task("vector", lambda e, h=h: e.tensor_tensor_scan(
                pc64[:, 4 * h:4 * h + 4], bd_own[:, 4 * h:4 * h + 4],
                bd_own[:, 4 * h:4 * h + 4], 0.0, ALU.add, ALU.bypass))
        p.task("vector", lambda e: e.tensor_tensor(
            v3(pc64, 4), v3(pc64, 4), bc3(pb16, 4), ALU.add))
        p.task("vector", lambda e: e.tensor_scalar(
            cd64[:, :], pc64[:, :], 0.0, -80.0, ALU.min, ALU.max))
        p.task("vector", lambda e: e.tensor_tensor(
            v3(cd64, 4), v3(cd64, 4), bc3(stab16, 4), ALU.subtract))
        nct = p.task("vector", lambda e: e.tensor_scalar(
            cd64[:, :], cd64[:, :], 0.0, -20.0, ALU.min, ALU.max))
        p.task("scalar", lambda e: e.activation(E64[:, :], cd64[:, :], AF.Exp),
               after=[nct])
        ext = p.task("scalar", lambda e: e.activation(
            En64[:, :], cd64[:, :], AF.Exp, scale=-1.0))
        p.task("vector", lambda e: e.tensor_tensor(
            sd64[:, :], bo_own[:, :], En64[:, :], ALU.mult), after=[ext])
        for h in range(NH):
            p.task("vector", lambda e, h=h: e.tensor_tensor_scan(
                si64[:, 4 * h:4 * h + 4], sd64[:, 4 * h:4 * h + 4],
                sd64[:, 4 * h:4 * h + 4], 0.0, ALU.add, ALU.bypass))
        p.task("vector", lambda e: e.tensor_copy(
            v3(sh64, 4)[:, :, 1:4], v3(si64, 4)[:, :, 0:3]))
        p.task("vector", lambda e: e.tensor_copy(sh64[:, 0:64:4], sb16[:, :]))
        p.task("vector", lambda e: e.tensor_tensor(
            v3(sh64, 4)[:, :, 1:4], v3(sh64, 4)[:, :, 1:4], bc3(sb16, 3), ALU.add))
        cat = p.task("vector", lambda e: e.tensor_tensor(
            ca64[:, :], sh64[:, :], E64[:, :], ALU.mult))
        p.guard = False
        fin_tok = []
        lco = [None, None]
        lel = [None, None]
        for h in range(NH):
            hp = h % 2
            lco[hp] = p.task("sync", lambda e, h=h, hp=hp: e.dma_start(
                out=co_t[hp][:, :], in_=scr_co[:, h * TC:(h + 1) * TC]),
                after=fin_tok[-2:-1] if len(fin_tok) >= 2 else [])
            lel[hp] = p.task("sync", lambda e, h=h, hp=hp: e.dma_start(
                out=el_t[hp][:, :], in_=scr_el[:, h * TC:(h + 1) * TC]))
            p.task("vector", lambda e, h=h, hp=hp: e.tensor_tensor(
                v3(el_t[hp], CH)[:, :, :].opt(),
                el_t[hp][:, :].rearrange("p (a b) -> p a b", b=CH),
                ca64[:, 4 * h:4 * h + 4].rearrange("p (a b) -> p a b", b=1)
                .to_broadcast([128, 4, CH]), ALU.mult), after=[lel[hp], cat])
            ft = p.task("vector", lambda e, h=h, hp=hp: e.tensor_tensor(
                fin[:, h * TC:(h + 1) * TC], el_t[hp][:, :], co_t[hp][:, :], ALU.add),
                after=[lco[hp]])
            fin_tok.append(ft)
        arf.pop()
        if tap:
            p.task("sync", lambda e: e.dma_start(out=tp["t_fin"][:, :], in_=fin[:, :]),
                   after=fin_tok[-1:])

        arf.push()
        gz_t = [arf.alloc(TC), arf.alloc(TC)]
        wtok = [None, None]
        wuse = [None, None]
        vmg_tok = [None]
        lgz = [fin_tok[-1], fin_tok[-1]]
        sg = [None, None]
        for fo in range(16):
            par = fo % 2
            wtok[par] = p.task("gdma", lambda e, par=par, fo=fo: e.dma_start(
                out=wg[par][:, 0:2048], in_=di["w_mb"][:, fo * 2048:(fo + 1) * 2048]),
                after=[wuse[par]])
            lgz[par] = p.task("sync", lambda e, fo=fo, par=par: e.dma_start(
                out=gz_t[par][:, :], in_=scr_gz[:, fo * TC:(fo + 1) * TC]),
                after=[lgz[par], vmg_tok[-2] if len(vmg_tok) >= 2 else None])
            sg[par] = p.task("scalar", lambda e, par=par: e.activation(
                gz_t[par][:, :], gz_t[par][:, :], AF.Sigmoid), after=[lgz[par]])
            for tc2 in range(2):
                def mmv(e, par=par, tc2=tc2):
                    for kt in range(KT):
                        r = e.matmul(PS[2 + tc2][:, :],
                                     wg[par][:, kt * 128:(kt + 1) * 128],
                                     fin[:, kt * TC + tc2 * 512:kt * TC + tc2 * 512 + 512],
                                     start=(kt == 0), stop=(kt == KT - 1))
                    return r
                mm = p.task("tensor", mmv,
                            after=[wtok[par], fin_tok[-1], vmg_tok[-2] if len(vmg_tok) >= 2 else None])
                vt = p.task("vector", lambda e, fo=fo, tc2=tc2, par=par: e.tensor_tensor(
                    vmg[:, fo * TC + tc2 * 512:fo * TC + tc2 * 512 + 512],
                    PS[2 + tc2][:, :], gz_t[par][:, tc2 * 512:(tc2 + 1) * 512], ALU.mult),
                    after=[mm, sg[par]])
                vmg_tok.append(vt)
                wuse[par] = vt
        vmg_done = vmg_tok[-1]
        arf.pop()
        if tap:
            p.task("sync", lambda e: e.dma_start(out=tp["t_vmg"][:, :], in_=vmg[:, :]),
                   after=[vmg_done])

        arf.push()
        x_t = [arf.alloc(TC), arf.alloc(TC)]
        xn_t = [arf.alloc(TC), arf.alloc(TC)]
        wuse = [vmg_done, vmg_done]
        lx = [vmg_done, vmg_done]
        out_done = [None, None]
        ydma = [None, None]
        for fo in range(16):
            par = fo % 2
            wtok[par] = p.task("gdma", lambda e, par=par, fo=fo: e.dma_start(
                out=wd[par][:, 0:1408], in_=di["w_out"][:, fo * 2048:fo * 2048 + 1408]),
                after=[wuse[par]])
            wtok2 = p.task("gdma", lambda e, par=par, fo=fo: e.dma_start(
                out=wg[par][:, 0:640], in_=di["w_out"][:, fo * 2048 + 1408:(fo + 1) * 2048]))
            lx[par] = p.task("sync", lambda e, fo=fo, par=par: e.dma_start(
                out=x_t[par][:, :], in_=x3[:, fo, 3:XC]), after=[lx[par], wuse[par]])
            x1 = p.task("gpsimd", lambda e, par=par: e.tensor_tensor(
                xn_t[par][:, :], x_t[par][:, :], mb_t[:, :], ALU.subtract),
                after=[lx[par], bdone, ydma[par]])
            x2 = p.task("gpsimd", lambda e, par=par: e.tensor_tensor(
                xn_t[par][:, :], xn_t[par][:, :], rb_t[:, :], ALU.mult))
            for tc2 in range(2):
                def mmo(e, par=par, tc2=tc2):
                    for kt in range(KT):
                        w = wd[par][:, kt * 128:(kt + 1) * 128] if kt < FT else \
                            wg[par][:, (kt - FT) * 128:(kt - FT + 1) * 128]
                        r = e.matmul(PS[2 * par + tc2][:, :], w,
                                     vmg[:, kt * TC + tc2 * 512:kt * TC + tc2 * 512 + 512],
                                     start=(kt == 0), stop=(kt == KT - 1))
                    return r
                mm = p.task("tensor", mmo,
                            after=[wtok[par], wtok2, vmg_done, out_done[-4] if len(out_done) >= 4 else None])
                p.task("vector", lambda e, tc2=tc2, par=par: e.tensor_tensor(
                    xn_t[par][:, tc2 * 512:(tc2 + 1) * 512], PS[2 * par + tc2][:, :],
                    xn_t[par][:, tc2 * 512:(tc2 + 1) * 512], ALU.add), after=[mm, x2])
                p.task("vector", lambda e, fo=fo, tc2=tc2, par=par: e.tensor_tensor(
                    xn_t[par][:, tc2 * 512:(tc2 + 1) * 512],
                    xn_t[par][:, tc2 * 512:(tc2 + 1) * 512],
                    acc_cols[:, fo * TC + tc2 * 512:fo * TC + tc2 * 512 + 512], ALU.mult))
                s3 = p.task("vector", lambda e, tc2=tc2, par=par: e.tensor_tensor(
                    xn_t[par][:, tc2 * 512:(tc2 + 1) * 512],
                    xn_t[par][:, tc2 * 512:(tc2 + 1) * 512],
                    x_t[par][:, tc2 * 512:(tc2 + 1) * 512], ALU.add))
                out_done.append(s3)
            wuse[par] = out_done[-1]
            ydma[par] = p.task("sync", lambda e, fo=fo, par=par: e.dma_start(
                out=y_d[:, fo * TC:(fo + 1) * TC], in_=xn_t[par][:, :]), after=[out_done[-1]])
        arf.pop()
        arf.pop()
        p.guard = True

    phase_p0("xp", own=False)
    p.barrier()
    phase_front(own=False)
    p.barrier()
    phase_p0("xt", own=True)
    p.barrier()
    if tap:
        p.task("sync", lambda e: e.dma_start(out=tp["t_xnbf"][:, :], in_=xnbf[:, :]))
        p.task("sync", lambda e: e.dma_start(out=tp["t_xc"][:, :], in_=xcbf[:, :]))
        p.task("sync", lambda e: e.dma_start(out=tp["t_rows"][:, 0:XC], in_=m_row[:, :]))
        p.task("sync", lambda e: e.dma_start(out=tp["t_rows"][:, XC:2 * XC], in_=r_row[:, :]))
    phase_front(own=True)
    p.barrier()
    if tap:
        p.task("sync", lambda e: e.dma_start(out=tp["t_bdbo"][:, 0:64], in_=bd_own[:, :]))
        p.task("sync", lambda e: e.dma_start(out=tp["t_bdbo"][:, 64:128], in_=bo_own[:, :]))
        p.task("sync", lambda e: e.dma_start(out=tp["t_bdbo"][:, 128:176], in_=pstat[:, :]))
    arf.pop()
    acc_cols = arf.alloc(KT * TC)
    phase_ffn(acc_cols)
    p.barrier()
    phase_tail(acc_cols)


def _tile_k(w):
    c = w.shape[1]
    return np.ascontiguousarray(
        w.reshape(16, 128, c).transpose(1, 0, 2).reshape(128, 16 * c))


def _host_prep(inputs):
    f = np.float32
    x = np.asarray(inputs["L0_attn_x"], f)
    w_in = np.ascontiguousarray(
        np.asarray(inputs["L0_attn_in_proj_w"], f).T
        .reshape(16, 128, 3, 16, 128).transpose(1, 3, 0, 2, 4).reshape(128, -1))
    w_rt = _tile_k(np.ascontiguousarray(np.asarray(inputs["L0_ffn_W_router"], f).T))
    w_sg = _tile_k(np.ascontiguousarray(np.asarray(inputs["L0_ffn_W_sgate"], f).T))
    G = np.asarray(inputs["L0_ffn_Ws_gate_up"], f)
    ga = G[:, :5632].reshape(2048, SHB, FT, 128)
    up = G[:, 5632:].reshape(2048, SHB, FT, 128)
    st = np.stack([ga, up], axis=2)
    w_shgu = np.ascontiguousarray(
        st.reshape(16, 128, SHB, 2, FT, 128).transpose(1, 2, 4, 3, 0, 5).reshape(128, -1))
    Dn = np.asarray(inputs["L0_ffn_Ws_down"], f).reshape(SHB, FT, 128, 16, 128)
    w_shdn = np.ascontiguousarray(Dn.transpose(2, 0, 3, 1, 4).reshape(128, -1))
    Ge = np.asarray(inputs["L0_ffn_We_gate_up"], f)
    eg = Ge[:, :, :1408].reshape(NE, 16, 128, FT, 128)
    eu = Ge[:, :, 1408:].reshape(NE, 16, 128, FT, 128)
    est = np.stack([eg, eu], axis=3)
    w_egu = np.ascontiguousarray(est.transpose(2, 0, 4, 3, 1, 5).reshape(128, -1))
    De = np.asarray(inputs["L0_ffn_We_down"], f).reshape(NE, FT, 128, 16, 128)
    w_edn = np.ascontiguousarray(De.transpose(2, 0, 3, 1, 4).reshape(128, -1))
    hm = np.asarray(inputs["L0_attn_head_mix"], f)
    mb = np.zeros((16, 128, 16, 128), f)
    for i in range(128):
        mb[:, i, :, i] = hm
    w_mb = np.ascontiguousarray(
        mb.reshape(2048, 2048).reshape(16, 128, 16, 128).transpose(1, 2, 0, 3).reshape(128, -1))
    w_out = np.ascontiguousarray(
        np.asarray(inputs["L0_attn_out_proj_w"], f).T
        .reshape(16, 128, 16, 128).transpose(1, 2, 0, 3).reshape(128, -1))
    cwv = np.ascontiguousarray(
        np.asarray(inputs["L0_attn_conv_w"], f).reshape(16, 128, 4).transpose(1, 0, 2).reshape(128, 64))
    cbv = np.ascontiguousarray(np.asarray(inputs["L0_attn_conv_b"], f).reshape(16, 128).T)
    nscv = np.ascontiguousarray(np.asarray(inputs["L0_attn_norm_scale"], f).reshape(16, 128).T)
    dtbv = np.ascontiguousarray(np.asarray(inputs["L0_attn_dt_bias"], f).reshape(16, 128).T)
    iota8 = np.broadcast_to(np.arange(8, dtype=f), (128, 8)).copy()
    esel = np.zeros((8, 1024), f)
    for e_ in range(8):
        esel[e_, e_ * 128:(e_ + 1) * 128] = 1.0
    ident = np.eye(128, dtype=f)

    def rope_tables(t0):
        t = np.arange(t0, t0 + TC, dtype=np.float64)
        inv = (1.0 / 10000.0) ** (np.arange(0, 128, 2, dtype=np.float64) / 128.0)
        fr = np.outer(inv, t)
        cos = np.cos(fr); sin = np.sin(fr)
        CT = np.concatenate([cos, cos], 0)
        ST = np.concatenate([-sin, sin], 0)
        out = []
        for tab in (CT, ST):
            hi = tab.astype(f)
            lo = (tab - hi.astype(np.float64)).astype(f)
            out.append((hi, lo))
        return out

    shared = dict(w_in=w_in, w_rt=w_rt, w_sg=w_sg, w_shgu=w_shgu, w_shdn=w_shdn,
                  w_egu=w_egu, w_edn=w_edn, w_mb=w_mb, w_out=w_out, cw=cwv,
                  cb=cbv, nsc=nscv, dtb=dtbv, iota8=iota8, ident=ident, esel=esel)

    def xfm(xseq, halo):
        xa = np.concatenate([halo, xseq], 0)
        return np.ascontiguousarray(
            xa.T.reshape(16, 128, XC).transpose(1, 0, 2).reshape(128, -1))

    rt = [rope_tables(0), rope_tables(1024)]
    in_maps = []
    for c in range(8):
        b, hh = c // 2, c % 2
        start = hh * 1024
        halo = x[b, start - 3:start] if hh == 1 else np.zeros((3, 2048), f)
        xt = xfm(x[b, start:start + 1024], halo)
        if hh == 1:
            xp = xfm(x[b, 0:1024], np.zeros((3, 2048), f))
        else:
            xp = np.zeros((128, KT * XC), f)
        (cth, ctl), (sth, stl) = rt[hh]
        (pch, pcl), (psh, psl) = rt[0]
        rope = np.concatenate([cth, sth, pch, psh, ctl, stl, pcl, psl], 1)
        m = np.full((128, 1), 1.0 if hh == 1 else 0.0, f)
        im = dict(shared)
        im.update(xt=xt, xp=xp, rope=np.ascontiguousarray(rope), mflag=m)
        in_maps.append(im)
    return in_maps


_NC_CACHE = {}


def kernel(**inputs):
    if "nc" not in _NC_CACHE:
        _NC_CACHE["nc"] = build_nc(tap=False)
    nc = _NC_CACHE["nc"]
    in_maps = _host_prep(inputs)
    res = run_bass_kernel_spmd(nc, in_maps, core_ids=list(range(8)))
    out = np.empty((4, 2048, 2048), np.float32)
    for c in range(8):
        b, hh = c // 2, c % 2
        y = np.asarray(res.results[c]["y"]).reshape(128, 16, TC)
        out[b, hh * 1024:(hh + 1) * 1024] = y.transpose(2, 1, 0).reshape(TC, 2048)
    return out
